# version 01 (final)
# baseline (speedup 1.0000x reference)
import contextlib
import numpy as np
import concourse.bass as bass
import concourse.mybir as mybir
from concourse.bass_utils import run_bass_kernel_spmd

F32 = mybir.dt.float32
BF16 = mybir.dt.bfloat16
I32 = mybir.dt.int32
AF = mybir.ActivationFunctionType
ALU = mybir.AluOpType
AX = mybir.AxisListType

NCORES = 8
D = 2048
B = 2
S = 16384
SLOC = S // NCORES
TOK = B * SLOC
EPS = 1e-5
GH = 5504
GS = GH // NCORES
DS = D // NCORES

ENGINES = ("sync", "gpsimd", "scalar", "vector", "tensor")


class Prog:

    SERIAL = False
    SERIAL_COMPUTE = False

    def __init__(self, nc, es):
        self.nc = nc
        self.es = es
        self.sems = {}
        self.cnt = {}
        self.steps = {e: [] for e in ENGINES}
        self._seq_total = 0
        self._alias = {}
        self._last_seq_tok = None
        self._dma_n = {}
        self._prev_compute = {}

    def _sem(self, name):
        if name not in self.sems:
            self.sems[name] = self.es.enter_context(self.nc.semaphore(name))
            self.cnt[name] = 0
        return self.sems[name]

    def step(self, eng, fn, waits=(), inc=None, inc_amt=1):
        if Prog.SERIAL:
            self._sem("SEQ")
            is_dma = inc_amt == 16
            amt = 16 if is_dma else 1
            prev = self._seq_total
            self._seq_total += amt
            tok = ("SEQ", self._seq_total)
            if inc is not None:
                self._sem(inc)
                self.cnt[inc] += inc_amt
            self.steps[eng].append((((("SEQ", prev),) if prev else ()),
                                    fn, "SEQ", amt))
            return tok
        if eng in ("vector", "scalar", "gpsimd") and inc_amt == 1 and not Prog.SERIAL_COMPUTE:
            if inc is None:
                inc = {"vector": "DV", "scalar": "AC", "gpsimd": "GP"}[eng]
            prev = self._prev_compute.get(eng)
            if prev is not None:
                waits = list(waits) + [prev]
            self._sem(inc)
            self.cnt[inc] += 1
            tok = (inc, self.cnt[inc])
            self._prev_compute[eng] = tok
            self.steps[eng].append((tuple(self._alias.get(x, x) for x in waits if x is not None),
                                    fn, inc, 1))
            return tok
        if Prog.SERIAL_COMPUTE and inc_amt == 1:
            self._sem("SEQ")
            self._seq_total += 1
            tok = ("SEQ", self._seq_total)
            named_tok = None
            if inc is not None:
                self.cnt[inc] = self.cnt.get(inc, 0) + 1
                named_tok = (inc, self.cnt[inc])
                self._alias[named_tok] = tok
            w = [self._alias.get(x, x) for x in waits if x is not None]
            if self._last_seq_tok is not None:
                w.append(self._last_seq_tok)
            self._last_seq_tok = tok
            self.steps[eng].append((tuple(w), fn, "SEQ", 1))
            return tok
        tok = None
        if inc is not None:
            self._sem(inc)
            self.cnt[inc] += inc_amt
            tok = (inc, self.cnt[inc])
        self.steps[eng].append((tuple(self._alias.get(x, x) for x in waits if x is not None),
                                fn, inc, inc_amt))
        return tok

    NSEM = 8

    def dma(self, eng, out, in_, waits=(), inc=None, **kw):
        n = self._dma_n.get(eng, 0)
        self._dma_n[eng] = n + 1
        s = f"Dq_{eng}_{n % Prog.NSEM}"
        self._sem(s)
        w = list(waits)
        if n >= Prog.NSEM:
            w.append((s, 16 * (n // Prog.NSEM)))
        self.cnt[s] += 16
        tok = (s, self.cnt[s])
        self.steps[eng].append((tuple(self._alias.get(x, x) for x in w if x is not None),
                                lambda e: e.dma_start(out=out, in_=in_, **kw),
                                s, 16))
        return tok

    def realize(self, block):
        for eng_name in ENGINES:
            steps = self.steps[eng_name]
            if not steps:
                continue
            deco = getattr(block, eng_name)

            def make_body(steps=steps):
                def body(e):
                    for waits, fn, inc, amt in steps:
                        merged = {}
                        for s, v in waits:
                            if v > merged.get(s, -1):
                                merged[s] = v
                        for s, v in merged.items():
                            if v > 0:
                                e.wait_ge(self.sems[s], v)
                        inst = fn(e)
                        if inc is not None:
                            if inst is None:
                                inst = e.nop()
                            inst.then_inc(self.sems[inc], amt)
                return body
            deco(make_body())


def wtile_ap(wb, kchunk_rows, col0, ncols):
    raise NotImplementedError



def build_ffn0(H=8192, tok=TOK, d=D, debug=False):
    nc = bass.Bass("TRN2", target_bir_lowering=False)
    KD = d // 128
    KH = H // 128
    TH = tok // 2
    NT = TH // 512

    xT = nc.dram_tensor("xT", [d, tok], F32, kind="ExternalInput").ap()
    w0T = nc.dram_tensor("w0T", [d, H], F32, kind="ExternalInput").ap()
    b0r = nc.dram_tensor("b0r", [128, KH], F32, kind="ExternalInput").ap()
    w1T = nc.dram_tensor("w1T", [H, d], F32, kind="ExternalInput").ap()
    b1r = nc.dram_tensor("b1r", [128, KD], F32, kind="ExternalInput").ap()
    x1T = nc.dram_tensor("x1T", [d, tok], F32, kind="ExternalOutput").ap()
    dbg = {}
    if debug:
        dbg["xn"] = nc.dram_tensor("dbg_xn", [d, tok // 2], BF16, kind="ExternalOutput").ap()
        dbg["h"] = nc.dram_tensor("dbg_h", [H, tok // 2], BF16, kind="ExternalOutput").ap()
        dbg["e2"] = nc.dram_tensor("dbg_e2", [d, tok], F32, kind="ExternalOutput").ap()

    w0b = nc.dram_tensor("w0b", [d, H], BF16).ap()
    w1b = nc.dram_tensor("w1b", [H, d], BF16).ap()
    hbuf = nc.dram_tensor("hbuf", [H, TH], BF16).ap()

    with contextlib.ExitStack() as es:
        sb = lambda n, sh, dt: es.enter_context(nc.sbuf_tensor(n, sh, dt))
        xn_s = sb("xn_s", [128, KD * TH], BF16)
        x_s = sb("x_s", [128, 2 * TH], F32)
        xb_s = sb("xb_s", [128, TH], BF16)
        sq_s = sb("sq_s", [128, TH], BF16)
        tmp_s = sb("tmp_s", [128, TH], F32)
        ones_s = sb("ones_s", [128, 128], BF16)
        m_s = sb("m_s", [128, TH], F32)
        inv_s = sb("inv_s", [128, TH], F32)
        var_s = sb("var_s", [128, TH], F32)
        w0g_s = sb("w0g_s", [128, 2 * KD * 128], BF16)
        w1g_s = sb("w1g_s", [128, 2 * 2 * 128], BF16)
        h_s = sb("h_s", [128, 2 * TH], BF16)
        ev_s = sb("ev_s", [128, TH], BF16)
        t1_s = sb("t1_s", [128, 2 * 512], F32)
        xr_s = sb("xr_s", [128, 4 * 512], F32)
        o_s = sb("o_s", [128, 4 * 512], F32)
        b0_s = sb("b0_s", [128, KH], F32)
        eps_s = sb("eps_s", [128, 1], F32)
        b1h_s = sb("b1h_s", [128, KD], F32)

        ps = es.enter_context(nc.psum_tensor("ps", [128, 4096], F32))

        p = Prog(nc, es)

        t_b0 = p.dma("sync", b0_s[:], b0r)
        t_b1 = p.dma("sync", b1h_s[:], b1r)
        t_b1h = p.step("vector", lambda e: e.tensor_scalar(
            b1h_s[:], b1h_s[:], 0.5, None, ALU.mult), waits=[t_b1], inc="DV")
        t_ones = p.step("gpsimd", lambda e: e.memset(ones_s[:], 1.0), inc="GP")
        t_eps = p.step("gpsimd", lambda e: e.memset(eps_s[:], float(EPS)), inc="GP")
        tw0l = [p.dma("gpsimd", w0b[c * (d // 8):(c + 1) * (d // 8), :],
                      w0T[c * (d // 8):(c + 1) * (d // 8), :]) for c in range(8)]
        tw1l = [p.dma("gpsimd", w1b[c * (H // 8):(c + 1) * (H // 8), :],
                      w1T[c * (H // 8):(c + 1) * (H // 8), :]) for c in range(8)]

        t_phase = None
        phase_guards = []
        xn_last_tok = None
        inv_tok = None
        for th in range(2):
            co = th * TH
            t_pe_prev = None
            prev_stores = phase_guards[-1][-Prog.NSEM:] if phase_guards else []
            prev_xn_last = xn_last_tok
            prev_inv_tok = inv_tok
            xdma = [None] * KD
            stat_mm = [None] * KD
            xbw = [None] * KD
            sqw = [None] * KD
            for k in range(KD):
                xbuf = x_s[:, (k % 2) * TH:(k % 2) * TH + TH]
                guard = xbw[k - 2] and sqw[k - 2] if k >= 2 else None
                xdma[k] = p.dma("sync", xbuf,
                                xT[k * 128:(k + 1) * 128, co:co + TH],
                                waits=[sqw[k - 2], xbw[k - 2]] if k >= 2 else [])
                xbw[k] = p.step("scalar", lambda e, xb=xbuf: e.activation(
                    out=xb_s[:], in_=xb, func=AF.Copy),
                    waits=[xdma[k], stat_mm[k - 1]], inc="AC")
                sqw[k] = p.step("scalar", lambda e, xb=xbuf: e.activation(
                    out=sq_s[:], in_=xb, func=AF.Square),
                    waits=[xdma[k]], inc="AC")
                last = None
                for n in range(NT):
                    p.step("tensor", lambda e, k=k, n=n: e.matmul(
                        ps[:, n * 512:(n + 1) * 512], ones_s[:],
                        xb_s[:, n * 512:(n + 1) * 512],
                        start=(k == 0), stop=(k == KD - 1)),
                        waits=([xbw[k], t_ones] + prev_stores) if k == 0 else [xbw[k]],
                        inc="PE")
                    last = p.step("tensor", lambda e, k=k, n=n: e.matmul(
                        ps[:, 2048 + n * 512:2048 + (n + 1) * 512], ones_s[:],
                        sq_s[:, n * 512:(n + 1) * 512],
                        start=(k == 0), stop=(k == KD - 1)),
                        waits=[sqw[k]], inc="PE")
                stat_mm[k] = last
            t_m = p.step("vector", lambda e: e.tensor_scalar(
                m_s[:], ps[:, 0:TH], 1.0 / d, None, ALU.mult),
                waits=[stat_mm[KD - 1], prev_xn_last], inc="DV")
            t_msq = p.step("scalar", lambda e: e.activation(
                out=var_s[:], in_=m_s[:], func=AF.Square),
                waits=[t_m, prev_inv_tok], inc="AC")
            t_var = p.step("vector", lambda e: e.scalar_tensor_tensor(
                var_s[:], ps[:, 2048:2048 + TH], 1.0 / d, var_s[:],
                ALU.mult, ALU.subtract), waits=[t_msq], inc="DV")
            t_ln = p.step("scalar", lambda e: e.activation(
                out=inv_s[:], in_=var_s[:], func=AF.Ln, bias=eps_s[:, 0:1]),
                waits=[t_var, t_eps], inc="AC")
            t_inv = p.step("scalar", lambda e: e.activation(
                out=inv_s[:], in_=inv_s[:], func=AF.Exp, scale=-0.5),
                waits=[t_ln], inc="AC")
            xn_done = [None] * KD
            for k in range(KD):
                xbuf = x_s[:, (k % 2) * TH:(k % 2) * TH + TH]
                t_x = p.dma("sync", xbuf,
                            xT[k * 128:(k + 1) * 128, co:co + TH],
                            waits=[xn_done[k - 2]] if k >= 2 else [t_inv])
                t_sub = p.step("vector", lambda e, xb=xbuf: e.tensor_sub(
                    tmp_s[:], xb, m_s[:]),
                    waits=[t_x, t_inv] + ([xn_done[k - 1]] if k >= 1 else []), inc="DV")
                xn_done[k] = p.step("vector", lambda e, k=k: e.tensor_mul(
                    xn_s[:, k * TH:(k + 1) * TH], tmp_s[:], inv_s[:]),
                    waits=[t_sub], inc="DV")
            t_xn = xn_done[KD - 1]
            xn_last_tok = t_xn
            inv_tok = t_inv
            mm1_done = [None] * KH
            ev_done = [None] * KH
            h_st = [None] * KH
            for m in range(KH):
                slot = m % 2
                wbuf = w0g_s[:, slot * KD * 128:(slot + 1) * KD * 128]
                src = bass.AP(w0b.tensor, m * 128,
                              [[H, 128], [128 * H, KD], [1, 128]])
                t_w = p.dma("gpsimd", wbuf.rearrange("p (kc o) -> p kc o", kc=KD),
                            src, waits=tw0l + ([mm1_done[m - 2]] if m >= 2 else []))
                mm_last = None
                for k in range(KD):
                    for n in range(NT):
                        isl = (k == KD - 1 and n == NT - 1)
                        mm_last = p.step("tensor", lambda e, k=k, n=n, wbuf=wbuf: e.matmul(
                            ps[:, n * 512:(n + 1) * 512],
                            wbuf[:, k * 128:(k + 1) * 128],
                            xn_s[:, k * TH + n * 512:k * TH + (n + 1) * 512],
                            start=(k == 0), stop=(k == KD - 1)),
                            waits=([t_w, t_xn, t_m, t_var] +
                                   ([ev_done[m - 1]] if m >= 1 else [])) if (k == 0 and n == 0) else [],
                            inc="PE" if isl else None)
                mm1_done[m] = mm_last
                ev_done[m] = p.step("scalar", lambda e, m=m: e.activation(
                    out=ev_s[:], in_=ps[:, 0:TH], func=AF.Gelu,
                    bias=b0_s[:, m:m + 1]),
                    waits=[mm_last, t_b0] + ([h_st[m - 1]] if m >= 1 else []),
                    inc="AC")
                h_st[m] = p.dma("sync", hbuf[m * 128:(m + 1) * 128, :], ev_s[:],
                                waits=[ev_done[m]])
            NXO = 4
            prev_guard = ev_done[KH - 1]
            o_hist = []
            o_done2 = []
            mm2_done = {}
            for blk in range(KD // 2):
                o2s = [blk * 2, blk * 2 + 1]
                mm_last = None
                for k2 in range(KH):
                    wslot = k2 % 2
                    wbuf = w1g_s[:, wslot * 256:(wslot + 1) * 256]
                    hslot = k2 % 2
                    hb = h_s[:, hslot * TH:(hslot + 1) * TH]
                    wg = tw1l + ([mm2_done[(blk, k2 - 2)]] if k2 >= 2 else
                                  ([mm2_done[(blk - 1, KH - 1)]] if blk >= 1 else []))
                    t_wl = p.dma("gpsimd",
                                 wbuf,
                                 w1b[k2 * 128:(k2 + 1) * 128, blk * 256:(blk + 1) * 256],
                                 waits=wg)
                    t_hl = p.dma("scalar", hb, hbuf[k2 * 128:(k2 + 1) * 128, :],
                                 waits=[h_st[k2]] + wg[8:])
                    for i in range(2):
                        for n in range(NT):
                            slotp = i * NT + n
                            isl = (i == 1 and n == NT - 1)
                            mm_last = p.step("tensor", lambda e, i=i, n=n, k2=k2, slotp=slotp, wbuf=wbuf, hb=hb: e.matmul(
                                ps[:, slotp * 512:(slotp + 1) * 512],
                                wbuf[:, i * 128:(i + 1) * 128],
                                hb[:, n * 512:(n + 1) * 512],
                                start=(k2 == 0), stop=(k2 == KH - 1)),
                                waits=([t_wl, t_hl, prev_guard] if (i == 0 and n == 0) else []),
                                inc="PE" if isl else None)
                    mm2_done[(blk, k2)] = mm_last
                last_ac = None
                for i, o2 in enumerate(o2s):
                    for n in range(NT):
                        slotp = i * NT + n
                        j = len(o_hist)
                        xrb = xr_s[:, (j % NXO) * 512:(j % NXO) * 512 + 512]
                        orb = o_s[:, (j % NXO) * 512:(j % NXO) * 512 + 512]
                        t_xr = p.dma("sync", xrb,
                                     xT[o2 * 128:(o2 + 1) * 128, co + n * 512:co + (n + 1) * 512],
                                     waits=[o_hist[j - NXO]] if j >= NXO else [])
                        t1b = t1_s[:, (j % 2) * 512:(j % 2) * 512 + 512]
                        t_t1 = p.step("scalar", lambda e, slotp=slotp, o2=o2, t1b=t1b: e.activation(
                            out=t1b, in_=ps[:, slotp * 512:(slotp + 1) * 512],
                            func=AF.Identity, scale=0.5, bias=b1h_s[:, o2:o2 + 1]),
                            waits=[mm_last, t_b1h] + ([o_done2[j - 2]] if j >= 2 else []), inc="AC")
                        t_o = p.step("vector", lambda e, orb=orb, xrb=xrb, t1b=t1b: e.tensor_add(
                            orb, t1b, xrb),
                            waits=[t_t1, t_xr] + ([o_hist[j - NXO]] if j >= NXO else []),
                            inc="DV")
                        if debug:
                            p.dma("sync", dbg["e2"][o2 * 128:(o2 + 1) * 128, co + n * 512:co + (n + 1) * 512],
                                  t1b, waits=[t_t1], inc="ST")
                        t_st = p.dma("sync",
                                     x1T[o2 * 128:(o2 + 1) * 128, co + n * 512:co + (n + 1) * 512],
                                     orb, waits=[t_o])
                        o_hist.append(t_st)
                        o_done2.append(t_o)
                        last_ac = t_t1
                prev_guard = last_ac
            t_phase = None
            _ = o_hist
            if debug and th == 0:
                for k in range(KD):
                    p.dma("sync", dbg["xn"][k * 128:(k + 1) * 128, :],
                          xn_s[:, k * TH:(k + 1) * TH], waits=[t_xn], inc="ST")
                for m2 in range(KH):
                    p.dma("sync", dbg["h"][m2 * 128:(m2 + 1) * 128, :],
                          hbuf[m2 * 128:(m2 + 1) * 128, :], waits=[h_st[m2]], inc="ST")
                t_phase = None
            all_st = list(o_hist)
            phase_guards.append(all_st)

        final_waits = []
        for toks in phase_guards:
            final_waits += toks[-Prog.NSEM:]
        p.step("gpsimd", lambda e: e.nop(), waits=final_waits)

        block = es.enter_context(nc.Block())
        p.realize(block)
    return nc



CONV_DIM = 4352
INTER = 4096
HEADS = 32
CHUNK = 256


def build_l0pre(tok=TOK):
    nc = bass.Bass("TRN2", target_bir_lowering=False)
    KD = D // 128
    TH = tok // 2
    NT = TH // 512
    KB = CONV_DIM // 128

    xT = nc.dram_tensor("xT", [D, tok], F32, kind="ExternalInput").ap()
    ipwT = nc.dram_tensor("ipwT", [D, 8480], F32, kind="ExternalInput").ap()
    hBCp = nc.dram_tensor("hBCp", [CONV_DIM, tok], BF16, kind="ExternalOutput").ap()
    dtraw = nc.dram_tensor("dtraw", [HEADS, tok], F32, kind="ExternalOutput").ap()
    sgtm = nc.dram_tensor("sgtm", [tok, INTER], BF16, kind="ExternalOutput").ap()
    ipb = nc.dram_tensor("ipb", [D, 8480], BF16).ap()

    with contextlib.ExitStack() as es:
        sb = lambda n, sh, dt: es.enter_context(nc.sbuf_tensor(n, sh, dt))
        xn_s = sb("xn_s", [128, KD * TH], BF16)
        x_s = sb("x_s", [128, 2 * TH], F32)
        xb_s = sb("xb_s", [128, TH], BF16)
        sq_s = sb("sq_s", [128, TH], BF16)
        tmp_s = sb("tmp_s", [128, TH], F32)
        ones_s = sb("ones_s", [128, 128], BF16)
        m_s = sb("m_s", [128, TH], F32)
        inv_s = sb("inv_s", [128, TH], F32)
        var_s = sb("var_s", [128, TH], F32)
        wfg_s = sb("wfg_s", [128, 2 * KD * 128], BF16)
        wmv_s = sb("wmv_s", [128, 2 * INTER], BF16)
        ev_s = sb("ev_s", [128, 2 * TH], BF16)
        evd_s = sb("evd_s", [32, TH], F32)
        evg_s = sb("evg_s", [128, 2 * INTER], BF16)
        eps_s = sb("eps_s", [128, 1], F32)

        ps = es.enter_context(nc.psum_tensor("ps", [128, 4096], F32))
        p = Prog(nc, es)

        t_ones = p.step("gpsimd", lambda e: e.memset(ones_s[:], 1.0))
        t_eps = p.step("gpsimd", lambda e: e.memset(eps_s[:], float(EPS)))
        twl = [p.dma("gpsimd", ipb[c * (D // 8):(c + 1) * (D // 8), :],
                     ipwT[c * (D // 8):(c + 1) * (D // 8), :]) for c in range(8)]

        phase_guards = []
        xn_last_tok = None
        inv_tok = None
        for th in range(2):
            co = th * TH
            prev_stores = phase_guards[-1][-Prog.NSEM:] if phase_guards else []
            prev_xn_last = xn_last_tok
            prev_inv_tok = inv_tok
            xdma = [None] * KD
            stat_mm = [None] * KD
            xbw = [None] * KD
            sqw = [None] * KD
            for k in range(KD):
                xbuf = x_s[:, (k % 2) * TH:(k % 2) * TH + TH]
                xdma[k] = p.dma("sync", xbuf,
                                xT[k * 128:(k + 1) * 128, co:co + TH],
                                waits=[sqw[k - 2], xbw[k - 2]] if k >= 2 else [])
                xbw[k] = p.step("scalar", lambda e, xb=xbuf: e.activation(
                    out=xb_s[:], in_=xb, func=AF.Copy),
                    waits=[xdma[k], stat_mm[k - 1]], inc="AC")
                sqw[k] = p.step("scalar", lambda e, xb=xbuf: e.activation(
                    out=sq_s[:], in_=xb, func=AF.Square),
                    waits=[xdma[k]], inc="AC")
                last = None
                for n in range(NT):
                    p.step("tensor", lambda e, k=k, n=n: e.matmul(
                        ps[:, n * 512:(n + 1) * 512], ones_s[:],
                        xb_s[:, n * 512:(n + 1) * 512],
                        start=(k == 0), stop=(k == KD - 1)),
                        waits=([xbw[k], t_ones] + prev_stores) if k == 0 else [xbw[k]],
                        inc="PE")
                    last = p.step("tensor", lambda e, k=k, n=n: e.matmul(
                        ps[:, 2048 + n * 512:2048 + (n + 1) * 512], ones_s[:],
                        sq_s[:, n * 512:(n + 1) * 512],
                        start=(k == 0), stop=(k == KD - 1)),
                        waits=[sqw[k]], inc="PE")
                stat_mm[k] = last
            t_m = p.step("vector", lambda e: e.tensor_scalar(
                m_s[:], ps[:, 0:TH], 1.0 / D, None, ALU.mult),
                waits=[stat_mm[KD - 1], prev_xn_last], inc="DV")
            t_msq = p.step("scalar", lambda e: e.activation(
                out=var_s[:], in_=m_s[:], func=AF.Square),
                waits=[t_m, prev_inv_tok], inc="AC")
            t_var = p.step("vector", lambda e: e.scalar_tensor_tensor(
                var_s[:], ps[:, 2048:2048 + TH], 1.0 / D, var_s[:],
                ALU.mult, ALU.subtract), waits=[t_msq], inc="DV")
            t_ln = p.step("scalar", lambda e: e.activation(
                out=inv_s[:], in_=var_s[:], func=AF.Ln, bias=eps_s[:, 0:1]),
                waits=[t_var, t_eps], inc="AC")
            t_inv = p.step("scalar", lambda e: e.activation(
                out=inv_s[:], in_=inv_s[:], func=AF.Exp, scale=-0.5),
                waits=[t_ln], inc="AC")
            xn_done = [None] * KD
            for k in range(KD):
                xbuf = x_s[:, (k % 2) * TH:(k % 2) * TH + TH]
                t_x = p.dma("sync", xbuf,
                            xT[k * 128:(k + 1) * 128, co:co + TH],
                            waits=[xn_done[k - 2]] if k >= 2 else [t_inv])
                t_sub = p.step("vector", lambda e, xb=xbuf: e.tensor_sub(
                    tmp_s[:], xb, m_s[:]),
                    waits=[t_x, t_inv] + ([xn_done[k - 1]] if k >= 1 else []), inc="DV")
                xn_done[k] = p.step("vector", lambda e, k=k: e.tensor_mul(
                    xn_s[:, k * TH:(k + 1) * TH], tmp_s[:], inv_s[:]),
                    waits=[t_sub], inc="DV")
            t_xn = xn_done[KD - 1]
            xn_last_tok = t_xn
            inv_tok = t_inv
            mm_done = [None] * (KB + 1)
            ev_done = [None] * (KB + 1)
            st_done = [None] * (KB + 1)
            for m in range(KB + 1):
                slot = m % 2
                wbuf = wfg_s[:, slot * KD * 128:(slot + 1) * KD * 128]
                ncols = 128 if m < KB else 32
                src = bass.AP(ipb.tensor, INTER + m * 128,
                              [[8480, 128], [128 * 8480, KD], [1, ncols]])
                t_w = p.dma("gpsimd", wbuf.rearrange("p (kc o) -> p kc o", kc=KD)[:, :, 0:ncols],
                            src, waits=twl + ([mm_done[m - 2]] if m >= 2 else []))
                mm_last = None
                for k in range(KD):
                    for n in range(NT):
                        mm_last = p.step("tensor", lambda e, k=k, n=n, wbuf=wbuf, ncols=ncols: e.matmul(
                            ps[0:ncols, n * 512:(n + 1) * 512],
                            wbuf[:, k * 128:k * 128 + ncols],
                            xn_s[:, k * TH + n * 512:k * TH + (n + 1) * 512],
                            start=(k == 0), stop=(k == KD - 1)),
                            waits=([t_w, t_xn, t_m, t_var] +
                                   ([ev_done[m - 1]] if m >= 1 else [])) if (k == 0 and n == 0) else [],
                            inc="PE" if (k == KD - 1 and n == NT - 1) else None)
                mm_done[m] = mm_last
                if m < KB:
                    ebuf = ev_s[:, slot * TH:(slot + 1) * TH]
                    ev_done[m] = p.step("scalar", lambda e, ebuf=ebuf: e.activation(
                        out=ebuf, in_=ps[:, 0:TH], func=AF.Copy),
                        waits=[mm_last, st_done[m - 2]], inc="AC")
                    st_done[m] = p.dma("sync", hBCp[m * 128:(m + 1) * 128, co:co + TH],
                                       ebuf, waits=[ev_done[m]])
                else:
                    ev_done[m] = p.step("scalar", lambda e: e.activation(
                        out=evd_s[:], in_=ps[0:32, 0:TH], func=AF.Copy),
                        waits=[mm_last], inc="AC")
                    st_done[m] = p.dma("sync", dtraw[:, co:co + TH], evd_s[:],
                                       waits=[ev_done[m]])
            prev_ev = ev_done[KB]
            gev = {}
            gst = {}
            mkprev = {}
            for tc in range(TH // 128):
                mk = {}
                for k in range(KD):
                    wslot = k % 2
                    wbuf = wmv_s[:, wslot * INTER:(wslot + 1) * INTER]
                    wguard = [mk[k - 2]] if k >= 2 else \
                             ([mkprev[KD - 1]] if tc >= 1 else [])
                    t_wv = p.dma("gpsimd", wbuf, ipb[k * 128:(k + 1) * 128, 0:INTER],
                                 waits=twl + wguard)
                    for g in range(8):
                        mk[k] = p.step("tensor", lambda e, k=k, g=g, tc=tc, wbuf=wbuf: e.matmul(
                            ps[:, g * 512:(g + 1) * 512],
                            xn_s[:, k * TH + tc * 128:k * TH + tc * 128 + 128],
                            wbuf[:, g * 512:(g + 1) * 512],
                            start=(k == 0), stop=(k == KD - 1)),
                            waits=([t_wv, prev_ev, t_xn] if g == 0 else []),
                            inc="PE" if g == 7 else None)
                mkprev = mk
                gbuf = evg_s[:, (tc % 2) * INTER:(tc % 2) * INTER + INTER]
                gev[tc] = p.step("scalar", lambda e, gbuf=gbuf: e.activation(
                    out=gbuf, in_=ps[:, 0:INTER], func=AF.Silu),
                    waits=[mk[KD - 1], gst.get(tc - 2)], inc="AC")
                prev_ev = gev[tc]
                gst[tc] = p.dma("sync", sgtm[co + tc * 128:co + (tc + 1) * 128, :],
                                gbuf, waits=[gev[tc]])
            phase_guards.append([st_done[KB], gst[TH // 128 - 1], gst[TH // 128 - 2]] +
                                [st_done[KB - 1]])

        final_waits = []
        for toks in phase_guards:
            final_waits += toks
        p.step("gpsimd", lambda e: e.nop(), waits=final_waits)
        block = es.enter_context(nc.Block())
        p.realize(block)
    return nc



def build_l0mid(tok=TOK, phases=7):
    nc = bass.Bass("TRN2", target_bir_lowering=False)
    TH = tok // 2
    NC_ = TH // CHUNK
    KB = CONV_DIM // 128
    TC = TH // 128

    hBCh = nc.dram_tensor("hBCh", [CONV_DIM, 2 * (3 + TH)], BF16, kind="ExternalInput").ap()
    dtraw = nc.dram_tensor("dtraw", [HEADS, tok], F32, kind="ExternalInput").ap()
    cwr = nc.dram_tensor("cwr", [128, KB * 4], F32, kind="ExternalInput").ap()
    cbr = nc.dram_tensor("cbr", [128, KB], F32, kind="ExternalInput").ap()
    alog = nc.dram_tensor("alog", [HEADS, 1], F32, kind="ExternalInput").ap()
    dtb = nc.dram_tensor("dtb", [HEADS, 1], F32, kind="ExternalInput").ap()

    xstm = nc.dram_tensor("xstm", [tok, INTER], BF16, kind="ExternalOutput").ap()
    xdtm = nc.dram_tensor("xdtm", [tok, INTER], BF16, kind="ExternalOutput").ap()
    Bfm = nc.dram_tensor("Bfm", [128, tok], BF16, kind="ExternalOutput").ap()
    Cfm = nc.dram_tensor("Cfm", [128, tok], BF16, kind="ExternalOutput").ap()
    sdtm = nc.dram_tensor("sdtm", [tok, 32], BF16, kind="ExternalOutput").ap()
    ydiag = nc.dram_tensor("ydiag", [tok, INTER], BF16, kind="ExternalOutput").ap()
    dacc = nc.dram_tensor("dacc", [2, HEADS, NC_], F32, kind="ExternalOutput").ap()
    slocal = nc.dram_tensor("slocal", [2, NC_, 128, HEADS * 128], F32, kind="ExternalOutput").ap()
    send = nc.dram_tensor("send", [2, 128, HEADS * 128], F32, kind="ExternalOutput").ap()

    xsfm = nc.dram_tensor("xsfm", [INTER, tok], BF16).ap()
    drow = nc.dram_tensor("drow", [128, tok], BF16).ap()
    btm_d = nc.dram_tensor("btm_d", [tok, 128], BF16).ap()
    acum_d = nc.dram_tensor("acum_d", [32, tok], F32).ap()

    with contextlib.ExitStack() as es:
        sb = lambda n, sh, dt: es.enter_context(nc.sbuf_tensor(n, sh, dt))
        hb_s = sb("hb_s", [128, 2 * (3 + TH)], BF16)
        acc_s = sb("acc_s", [128, TH], F32)
        sx_s = sb("sx_s", [128, 2 * TH], BF16)
        cw_s = sb("cw_s", [128, KB * 4], F32)
        cb_s = sb("cb_s", [128, KB], F32)
        Bc_s = sb("Bc_s", [128, TH], BF16)
        Cc_s = sb("Cc_s", [128, TH], BF16)
        dt_s = sb("dt_s", [32, TH], F32)
        dw1_s = sb("dw1_s", [32, TH], F32)
        dw2_s = sb("dw2_s", [32, TH], F32)
        acum_s = sb("acum_s", [32, TH], F32)
        rst_s = sb("rst_s", [32, TH], F32)
        sdds_s = sb("sdds_s", [128, TH], BF16)
        dacc_s = sb("dacc_s", [32, NC_], F32)
        daccf_s = sb("daccf_s", [1, HEADS * NC_], F32)
        daccb_s = sb("daccb_s", [128, HEADS * NC_], F32)
        al_s = sb("al_s", [32, 4], F32)
        dtm_s = sb("dtm_s", [128, TC * 128], BF16)
        btm_s = sb("btm_s", [128, TC * 128], BF16)
        xdt_s = sb("xdt_s", [128, 2 * 512], BF16)
        xdc_s = sb("xdc_s", [128, 2 * INTER], BF16)
        xs_s = sb("xs_s", [128, 2 * 512], BF16)
        mask_s = sb("mask_s", [128, 2 * CHUNK], BF16)
        Gm_s = sb("Gm_s", [128, 2 * CHUNK], BF16)
        combo1 = sb("combo1", [2, 2 * 4 * CHUNK], F32)
        combo2 = sb("combo2", [2, 2 * 4 * CHUNK], F32)
        df_s = sb("df_s", [128, 2 * CHUNK], F32)
        lx_s = sb("lx_s", [128, 2 * CHUNK], BF16)
        M_s = sb("M_s", [128, 2 * CHUNK], BF16)
        bd_s = sb("bd_s", [128, 2 * 128], BF16)
        ydev_s = sb("ydev_s", [128, 2 * 512], BF16)
        stb_s = sb("stb_s", [128, 8 * 512], F32)
        P_s = sb("P_s", [128, HEADS * 128], F32)
        ones_s = sb("ones_s", [1, 128], F32)
        io_s = sb("io_s", [128, 2 * CHUNK], F32)

        ps = es.enter_context(nc.psum_tensor("ps", [128, 4096], F32))
        p = Prog(nc, es)

        t_cw = p.dma("sync", cw_s[:], cwr)
        t_cb = p.dma("sync", cb_s[:], cbr)
        t_al = p.dma("sync", al_s[:, 0:1], alog)
        t_db = p.dma("sync", al_s[:, 1:2], dtb)
        t_on = p.step("gpsimd", lambda e: e.memset(ones_s[:], 1.0))
        t_c1o = p.step("gpsimd", lambda e: e.memset(combo1[0:2, :], 1.0))
        t_c2o = p.step("gpsimd", lambda e: e.memset(combo2[0:2, :], -1.0))
        t_one = p.step("gpsimd", lambda e: e.memset(al_s[:, 2:3], 1.0))
        t_io = p.step("gpsimd", lambda e: e.iota(
            io_s[:, 0:CHUNK], pattern=[[1, CHUNK]], base=0, channel_multiplier=-1,
            allow_small_or_imprecise_dtypes=True))
        t_io2 = p.step("gpsimd", lambda e: e.iota(
            io_s[:, CHUNK:2 * CHUNK], pattern=[[1, CHUNK]], base=-128, channel_multiplier=-1,
            allow_small_or_imprecise_dtypes=True))
        t_mk0 = p.step("vector", lambda e: e.tensor_scalar(
            mask_s[:, 0:CHUNK], io_s[:, 0:CHUNK], 0.0, None, ALU.is_ge), waits=[t_io])
        t_mk1 = p.step("vector", lambda e: e.tensor_scalar(
            mask_s[:, CHUNK:2 * CHUNK], io_s[:, CHUNK:2 * CHUNK], 0.0, None, ALU.is_ge),
            waits=[t_io2])
        t_io3 = p.step("gpsimd", lambda e: e.iota(
            rst_s[:].rearrange("h (c t) -> h c t", t=CHUNK), pattern=[[0, NC_], [1, CHUNK]],
            base=0, channel_multiplier=0, allow_small_or_imprecise_dtypes=True))
        t_rst = p.step("vector", lambda e: e.tensor_scalar(
            rst_s[:], rst_s[:], 1.0, None, ALU.is_ge), waits=[t_io3])
        t_a1 = p.step("scalar", lambda e: e.activation(
            out=al_s[:, 3:4], in_=al_s[:, 0:1], func=AF.Exp), waits=[t_al], inc="AC")
        t_a2 = p.step("vector", lambda e: e.tensor_scalar(
            al_s[:, 3:4], al_s[:, 3:4], -1.0, None, ALU.mult), waits=[t_a1], inc="DV")

        prev_sex = None
        xl_hist = []
        for b in range(2):
            co = b * TH
            hco = b * (3 + TH)
            conv_done = [None] * KB
            st_done = [None] * KB
            for m in range(KB):
                slot = m % 2
                hbuf = hb_s[:, slot * (3 + TH):(slot + 1) * (3 + TH)]
                t_h = p.dma("sync", hbuf, hBCh[m * 128:(m + 1) * 128, hco:hco + 3 + TH],
                            waits=[conv_done[m - 2]] if m >= 2 else [])
                t_c0 = p.step("vector", lambda e, hbuf=hbuf, m=m: e.tensor_scalar(
                    acc_s[:], hbuf[:, 0:TH], cw_s[:, m * 4:m * 4 + 1], None, ALU.mult),
                    waits=[t_h, t_cw], inc="DV")
                tj = t_c0
                for j in range(1, 4):
                    tj = p.step("vector", lambda e, hbuf=hbuf, m=m, j=j: e.scalar_tensor_tensor(
                        acc_s[:], hbuf[:, j:j + TH], cw_s[:, m * 4 + j:m * 4 + j + 1],
                        acc_s[:], ALU.mult, ALU.add),
                        waits=[tj], inc="DV")
                sxb = sx_s[:, slot * TH:(slot + 1) * TH]
                conv_done[m] = p.step("scalar", lambda e, sxb=sxb, m=m: e.activation(
                    out=sxb, in_=acc_s[:], func=AF.Silu, bias=cb_s[:, m:m + 1]),
                    waits=[tj, t_cb, st_done[m - 2]], inc="AC")
                if m < 32:
                    st_done[m] = p.dma("sync", xsfm[m * 128:(m + 1) * 128, co:co + TH],
                                       sxb, waits=[conv_done[m]])
                elif m == 32:
                    t_bk = p.step("vector", lambda e, sxb=sxb: e.tensor_copy(Bc_s[:], sxb),
                                  waits=[conv_done[m]], inc="DV")
                    st_done[m] = p.dma("sync", Bfm[:, co:co + TH], sxb, waits=[conv_done[m]])
                else:
                    t_ck = p.step("vector", lambda e, sxb=sxb: e.tensor_copy(Cc_s[:], sxb),
                                  waits=[conv_done[m]], inc="DV")
                    st_done[m] = p.dma("sync", Cfm[:, co:co + TH], sxb, waits=[conv_done[m]])
            if phases < 2:
                bstores0 = [st_done[33], st_done[0]]
                bstores1 = [st_done[33], st_done[0]]
                ydst = {}
                continue
            t_dt0 = p.dma("sync", dt_s[:], dtraw[:, co:co + TH])
            t_d1 = p.step("vector", lambda e: e.tensor_scalar(
                dt_s[:], dt_s[:], al_s[:, 1:2], None, ALU.add), waits=[t_dt0, t_db], inc="DV")
            t_d2 = p.step("scalar", lambda e: e.activation(
                out=dw1_s[:], in_=dt_s[:], func=AF.Abs), waits=[t_d1], inc="AC")
            t_d3 = p.step("scalar", lambda e: e.activation(
                out=dw1_s[:], in_=dw1_s[:], func=AF.Exp, scale=-1.0), waits=[t_d2], inc="AC")
            t_d4 = p.step("scalar", lambda e: e.activation(
                out=dw1_s[:], in_=dw1_s[:], func=AF.Ln, bias=al_s[:, 2:3]),
                waits=[t_d3, t_one], inc="AC")
            t_d5 = p.step("scalar", lambda e: e.activation(
                out=dw2_s[:], in_=dt_s[:], func=AF.Relu), waits=[t_d1], inc="AC")
            t_d6 = p.step("vector", lambda e: e.tensor_add(dt_s[:], dw1_s[:], dw2_s[:]),
                          waits=[t_d4, t_d5], inc="DV")
            t_d7 = p.step("vector", lambda e: e.tensor_scalar(
                dt_s[:], dt_s[:], 100.0, None, ALU.min), waits=[t_d6], inc="DV")
            t_ad = p.step("vector", lambda e: e.tensor_scalar(
                dw1_s[:], dt_s[:], al_s[:, 3:4], None, ALU.mult), waits=[t_d7, t_a2], inc="DV")
            t_acum = p.step("vector", lambda e: e.tensor_tensor_scan(
                acum_s[:], rst_s[:], dw1_s[:], 0.0, ALU.mult, ALU.add),
                waits=[t_ad, t_rst], inc="DV")
            t_sdz = p.step("gpsimd", lambda e: e.memset(sdds_s[:], 0.0),
                           waits=[t_dr] if b == 1 else [], inc="GP")
            t_sdd0 = p.step("scalar", lambda e: e.activation(
                out=sdds_s[0:32, :], in_=dt_s[:], func=AF.Copy),
                waits=[t_d7, t_sdz], inc="AC")
            t_sd = p.step("scalar", lambda e: e.activation(
                out=sdds_s[32:64, :], in_=acum_s[:], func=AF.Exp), waits=[t_acum], inc="AC")
            t_dsl = []
            for c in range(NC_):
                t_dsl.append(p.step("scalar", lambda e, c=c: e.activation(
                    out=sdds_s[64:96, c * CHUNK:(c + 1) * CHUNK],
                    in_=acum_s[:, c * CHUNK:(c + 1) * CHUNK], func=AF.Exp, scale=-1.0,
                    bias=acum_s[:, (c + 1) * CHUNK - 1:(c + 1) * CHUNK]),
                    waits=[t_acum], inc="AC"))
            acv = bass.AP(acum_s.ap().tensor, acum_s.ap().offset + CHUNK - 1,
                          [list(acum_s.ap().ap[0]), [CHUNK, NC_]])
            t_dacc = p.step("scalar", lambda e: e.activation(
                out=dacc_s[:], in_=acv, func=AF.Exp), waits=[t_acum], inc="AC")
            t_dacs = p.dma("sync", dacc[b], dacc_s[:], waits=[t_dacc])
            t_acd = p.dma("sync", acum_d[:, co:co + TH], acum_s[:], waits=[t_acum])
            t_daccf = p.dma("sync", daccf_s[:],
                            bass.AP(dacc.tensor, b * HEADS * NC_,
                                    [[0, 1], [1, HEADS * NC_]]), waits=[t_dacs])
            t_dbc_mm = p.step("tensor", lambda e: e.matmul(
                ps[:, 3584:3584 + HEADS * NC_], ones_s[:], daccf_s[:],
                start=True, stop=True), waits=[t_daccf, t_on], inc="PE")
            t_dbc = p.step("vector", lambda e: e.tensor_copy(
                daccb_s[:], ps[:, 3584:3584 + HEADS * NC_]), waits=[t_dbc_mm], inc="DV")
            t_dr = p.dma("sync", drow[:, co:co + TH], sdds_s[:],
                         waits=[t_sdd0, t_sd, t_sdz] + t_dsl)
            if phases < 3:
                bstores0 = [t_dr, t_dacs]
                bstores1 = [t_dr, t_dacs]
                ydst = {}
                continue
            tp_done = {}
            xst_done = {}
            tp_lin = []
            xst_lin = []
            for tc in range(TC):
                t_t1 = p.dma("sync", dtm_s[:, tc * 128:(tc + 1) * 128],
                             drow[:, co + tc * 128:co + (tc + 1) * 128],
                             waits=[t_dr], transpose=True)
                t_t2 = p.dma("sync", btm_s[:, tc * 128:(tc + 1) * 128],
                             Bfm[:, co + tc * 128:co + (tc + 1) * 128],
                             waits=[st_done[32]], transpose=True)
                t_bst = p.dma("sync", btm_d[co + tc * 128:co + (tc + 1) * 128, :],
                              btm_s[:, tc * 128:(tc + 1) * 128], waits=[t_t2])
                t_sdt = p.dma("sync", sdtm[co + tc * 128:co + (tc + 1) * 128, :],
                              dtm_s[:, tc * 128 + 32:tc * 128 + 64], waits=[t_t1])
                for cg in range(8):
                    jlin = tc * 8 + cg
                    xb = xs_s[:, (jlin % 2) * 512:(jlin % 2) * 512 + 512]
                    g2 = ([tp_lin[jlin - 2], xst_lin[jlin - 2]] if jlin >= 2 else [])
                    t_t3 = p.dma("sync", xb,
                                 xsfm[cg * 512:(cg + 1) * 512, co + tc * 128:co + (tc + 1) * 128],
                                 waits=[st_done[cg * 4 + 3]] + g2,
                                 transpose=True)
                    t_xst = p.dma("sync", xstm[co + tc * 128:co + (tc + 1) * 128,
                                               cg * 512:(cg + 1) * 512], xb, waits=[t_t3])
                    xst_done[(tc, cg)] = t_xst
                    xst_lin.append(t_xst)
                    dtv = bass.AP(dtm_s.ap().tensor,
                                  dtm_s.ap().offset + tc * 128 + cg * 4,
                                  [list(dtm_s.ap().ap[0]), [1, 4], [0, 128]])
                    xdb = xdt_s[:, (jlin % 2) * 512:(jlin % 2) * 512 + 512]
                    t_xd = p.step("vector", lambda e, xb=xb, dtv=dtv, xdb=xdb: e.tensor_tensor(
                        xdb.rearrange("p (h d) -> p h d", h=4),
                        xb.rearrange("p (h d) -> p h d", h=4), dtv, ALU.mult),
                        waits=[t_t3, t_t1] + g2, inc="DV")
                    t_xdt = p.dma("sync", xdtm[co + tc * 128:co + (tc + 1) * 128,
                                               cg * 512:(cg + 1) * 512],
                                  xdb, waits=[t_xd])
                    tp_done[(tc, cg)] = t_xdt
                    tp_lin.append(t_xdt)
            if phases < 4:
                bstores0 = [tp_lin[-1], xst_lin[-1]]
                bstores1 = [tp_lin[-1], xst_lin[-1]]
                ydst = {}
                continue
            t_pz = p.step("gpsimd", lambda e: e.memset(P_s[:], 0.0),
                          waits=[prev_sex], inc="GP")
            prev_chain = t_pz
            chain_done = {}
            prev_yd = {}
            ydev_done = {}
            stev_done = {}
            ydlast = {}
            last_gm_reader = None
            prev_gmm = None
            ydst = {}
            last_ye = None
            last_se = {}
            ydst_last = {}
            last_min = {}
            prev_dbc_reader = None
            for c in range(NC_):
                xlw = list(tp_lin[2 * c * 8:(2 * c + 1) * 8]) + xl_hist[-2:]
                if c >= 1 or b >= 1:
                    xlw += [("PE", p.cnt["PE"])]
                xl0 = p.dma("scalar", xdc_s[:, 0:INTER],
                            xdtm[co + c * CHUNK:co + c * CHUNK + 128, :], waits=xlw)
                xl1 = p.dma("scalar", xdc_s[:, INTER:2 * INTER],
                            xdtm[co + c * CHUNK + 128:co + (c + 1) * CHUNK, :],
                            waits=list(tp_lin[(2 * c + 1) * 8:(2 * c + 2) * 8]) +
                                  xl_hist[-2:] + ([("PE", p.cnt["PE"])] if (c or b) else []))
                xl_hist += [xl0, xl1]
                t_gmm = {}
                t_gm = {}
                for sh in range(2):
                    t_gmm[sh] = p.step("tensor", lambda e, c=c, sh=sh: e.matmul(
                        ps[:, 1536 + sh * 512:1536 + sh * 512 + 256],
                        Bc_s[:, c * CHUNK + sh * 128:c * CHUNK + (sh + 1) * 128],
                        Cc_s[:, c * CHUNK:(c + 1) * CHUNK],
                        start=True, stop=True),
                        waits=[t_bk, t_ck, last_gm_reader], inc="PE")
                for sh in range(2):
                    t_gm[sh] = p.step("vector", lambda e, sh=sh: e.tensor_tensor(
                        Gm_s[:, sh * CHUNK:(sh + 1) * CHUNK],
                        ps[:, 1536 + sh * 512:1536 + sh * 512 + 256],
                        mask_s[:, sh * CHUNK:(sh + 1) * CHUNK], ALU.mult),
                        waits=[t_gmm[sh], t_mk0, t_mk1], inc="DV")
                prev_gmm = t_gmm[1]
                for hg in range(8):
                    first_in_group = True
                    cslot = hg % 2
                    if phases == 4.1:
                        continue
                    acsrc = bass.AP(acum_d.tensor, (hg * 4) * tok + co + c * CHUNK,
                                    [[0, 1], [tok, 4], [1, CHUNK]])
                    t_cb1 = p.dma("sync", combo1[0:1, cslot * 1024:(cslot + 1) * 1024]
                                  .rearrange("p (a b) -> p a b", a=4),
                                  acsrc, waits=[t_acd, ("PE", p.cnt["PE"])])
                    t_cb2 = p.dma("sync", combo2[1:2, cslot * 1024:(cslot + 1) * 1024]
                                  .rearrange("p (a b) -> p a b", a=4),
                                  acsrc, waits=[t_acd, ("PE", p.cnt["PE"])])
                    if phases == 4.2:
                        continue
                    for hh in range(4):
                        h = hg * 4 + hh
                        t_M = {}
                        for sh in range(2):
                            dslot = sh
                            if phases == 4.3 and (hh > 0 or sh > 0):
                                continue
                            if phases == 4.42 and sh > 0:
                                continue
                            t_dmm = p.step("tensor", lambda e, h=h, hh=hh, sh=sh, dslot=dslot, cslot=cslot: e.matmul(
                                ps[:, 2560 + dslot * 512:2560 + dslot * 512 + 256],
                                combo1[0:2, cslot * 1024 + hh * CHUNK + sh * 128:cslot * 1024 + hh * CHUNK + sh * 128 + 128],
                                combo2[0:2, cslot * 1024 + hh * CHUNK:cslot * 1024 + (hh + 1) * CHUNK],
                                start=True, stop=True),
                                waits=[t_cb1, t_cb2, t_c1o, t_c2o, t_dbc,
                                       last_min.get((c, 2 * (h - 1) + sh))], inc="PE")
                            if phases == 4.3:
                                last_min[(c, 2 * h + sh)] = t_dmm
                                continue
                            if phases == 4.41:
                                t_min = p.step("vector", lambda e, dslot=dslot: e.tensor_copy(
                                    df_s[:, dslot * CHUNK:(dslot + 1) * CHUNK],
                                    ps[:, 2560 + dslot * 512:2560 + dslot * 512 + 256]),
                                    waits=[t_dmm], inc="DV")
                            else:
                                t_min = p.step("vector", lambda e, dslot=dslot: e.tensor_scalar(
                                    df_s[:, dslot * CHUNK:(dslot + 1) * CHUNK],
                                    ps[:, 2560 + dslot * 512:2560 + dslot * 512 + 256],
                                    0.0, None, ALU.min), waits=[t_dmm], inc="DV")
                            last_min[(c, 2 * h + sh)] = t_min
                            if phases in (4.4, 4.41, 4.42):
                                continue
                            t_lx = p.step("scalar", lambda e, dslot=dslot: e.activation(
                                out=lx_s[:, dslot * CHUNK:(dslot + 1) * CHUNK],
                                in_=df_s[:, dslot * CHUNK:(dslot + 1) * CHUNK],
                                func=AF.Exp), waits=[t_min], inc="AC")
                            if phases == 4.5:
                                continue
                            t_M[sh] = p.step("vector", lambda e, sh=sh, dslot=dslot: e.tensor_tensor(
                                M_s[:, dslot * CHUNK:(dslot + 1) * CHUNK],
                                lx_s[:, dslot * CHUNK:(dslot + 1) * CHUNK],
                                Gm_s[:, sh * CHUNK:(sh + 1) * CHUNK], ALU.mult),
                                waits=[t_lx, t_gm[sh]], inc="DV")
                        last_gm_reader = t_M.get(1, t_gm[1])
                        if phases in (4.3, 4.4, 4.41, 4.42, 4.5):
                            prev_yd[(c, h)] = last_min.get((c, 2 * h + 1), last_min.get((c, 2 * h)))
                            continue
                        if phases == 4:
                            prev_yd[(c, h)] = t_M[1]
                            continue
                        for lh in range(2):
                            for sh in range(2):
                                ydmm = p.step("tensor", lambda e, h=h, hh=hh, lh=lh, sh=sh: e.matmul(
                                    ps[:, lh * 512:lh * 512 + 128],
                                    M_s[:, sh * CHUNK + lh * 128:sh * CHUNK + lh * 128 + 128],
                                    xdc_s[:, sh * INTER + h * 128:sh * INTER + (h + 1) * 128],
                                    start=(sh == 0), stop=(sh == 1)),
                                    waits=([t_M[0], t_M[1], xl0, xl1] +
                                           [ydev_done.get((c, h - 1, lh))]),
                                    inc="PE" if sh == 1 else None)
                            ydev_done[(c, h, lh)] = p.step("scalar", lambda e, h=h, hh=hh, lh=lh: e.activation(
                                out=ydev_s[:, lh * 512 + hh * 128:lh * 512 + (hh + 1) * 128],
                                in_=ps[:, lh * 512:lh * 512 + 128], func=AF.Copy),
                                waits=[ydmm, ydlast.get(lh)], inc="AC")
                        for sh in range(2):
                            dscol = bass.AP(dtm_s.ap().tensor,
                                            dtm_s.ap().offset + (2 * c + sh) * 128 + 64 + h,
                                            [list(dtm_s.ap().ap[0]), [0, 128]])
                            t_bd = p.step("vector", lambda e, sh=sh, c=c, dscol=dscol: e.tensor_tensor(
                                bd_s[:, sh * 128:(sh + 1) * 128],
                                btm_s[:, (2 * c + sh) * 128:(2 * c + sh + 1) * 128],
                                dscol, ALU.mult),
                                waits=[t_t1, t_t2, ("PE", p.cnt["PE"])], inc="DV")
                            stmm = p.step("tensor", lambda e, sh=sh, h=h, hh=hh: e.matmul(
                                ps[:, 1024:1024 + 128],
                                bd_s[:, sh * 128:(sh + 1) * 128],
                                xdc_s[:, sh * INTER + h * 128:sh * INTER + (h + 1) * 128],
                                start=(sh == 0), stop=(sh == 1)),
                                waits=[t_bd] + ([stev_done.get((c, h - 1))] if sh == 0 else []),
                                inc="PE" if sh == 1 else None)
                        stev_done[(c, h)] = p.step("vector", lambda e, h=h, hg=hg, hh=hh: e.tensor_copy(
                            stb_s[:, hg * 512 + hh * 128:hg * 512 + (hh + 1) * 128],
                            ps[:, 1024:1024 + 128]),
                            waits=[stmm, chain_done.get((c - 1, hg))], inc="DV")
                        prev_yd[(c, h)] = stev_done[(c, h)]
                        first_in_group = False
                    t_pe_g = ("PE", p.cnt["PE"])
                    if phases in (4, 4.1, 4.2, 4.3, 4.4, 4.41, 4.42, 4.5):
                        continue
                    for lh in range(2):
                        t_ys = p.dma("sync",
                                     ydiag[co + c * CHUNK + lh * 128:co + c * CHUNK + (lh + 1) * 128,
                                           hg * 512:(hg + 1) * 512],
                                     ydev_s[:, lh * 512:(lh + 1) * 512],
                                     waits=[ydev_done[(c, hg * 4 + 3, lh)],
                                            ydev_done[(c, hg * 4 + 2, lh)],
                                            ydev_done[(c, hg * 4 + 1, lh)],
                                            ydev_done[(c, hg * 4, lh)]])
                        ydst[(c, hg, lh)] = t_ys
                        ydlast[lh] = t_ys
                    last_se[(c, hg)] = stev_done[(c, hg * 4 + 3)]
                if phases < 7:
                    continue
                t_pex = p.dma("sync", slocal[b, c], P_s[:], waits=[prev_chain])
                for hg in range(8):
                    for hh in range(4):
                        h = hg * 4 + hh
                        tchain = p.step("vector", lambda e, h=h, hg=hg, c=c: e.scalar_tensor_tensor(
                            P_s[:, h * 128:(h + 1) * 128], P_s[:, h * 128:(h + 1) * 128],
                            daccb_s[:, h * NC_ + c:h * NC_ + c + 1],
                            stb_s[:, hg * 512 + (h % 4) * 128:hg * 512 + (h % 4 + 1) * 128],
                            ALU.mult, ALU.add),
                            waits=[t_pex, t_dbc, stev_done[(c, h)]], inc="DV")
                        chain_done[(c, hg)] = tchain
                        prev_chain = tchain
            if phases < 7:
                if b == 0:
                    bstores0 = [t_dacs]
                else:
                    bstores1 = [t_dacs]
                continue
            t_sex = p.dma("sync", send[b], P_s[:], waits=[prev_chain])
            prev_sex = t_sex
            if b == 0:
                bstores0 = [t_sex, t_dacs]
            else:
                bstores1 = [t_sex, t_dacs]
        fw = [t for t in ([ydst.get((NC_ - 1, hg, lh)) for hg in range(8) for lh in range(2)]
                           if (phases >= 5 and phases < 34 and phases == int(phases)) else []) if t]
        if phases >= 4 and phases != 7:
            fw += [("PE", p.cnt["PE"]), ("DV", p.cnt["DV"])]
        p.step("gpsimd", lambda e: e.nop(), waits=bstores0 + bstores1 + fw)
        block = es.enter_context(nc.Block())
        p.realize(block)
    return nc



def build_l0fin(tok=TOK):
    nc = bass.Bass("TRN2", target_bir_lowering=False)
    TH = tok // 2
    NC_ = TH // CHUNK
    TC = TH // 128

    ydiag = nc.dram_tensor("ydiag", [tok, INTER], BF16, kind="ExternalInput").ap()
    xstm = nc.dram_tensor("xstm", [tok, INTER], BF16, kind="ExternalInput").ap()
    sgtm = nc.dram_tensor("sgtm", [tok, INTER], BF16, kind="ExternalInput").ap()
    Cfm = nc.dram_tensor("Cfm", [128, tok], BF16, kind="ExternalInput").ap()
    sdtm = nc.dram_tensor("sdtm", [tok, 32], BF16, kind="ExternalInput").ap()
    x1T = nc.dram_tensor("x1T", [D, tok], F32, kind="ExternalInput").ap()
    opwT = nc.dram_tensor("opwT", [INTER, D], F32, kind="ExternalInput").ap()
    normw = nc.dram_tensor("normw", [1, INTER], F32, kind="ExternalInput").ap()
    dvec = nc.dram_tensor("dvec", [1, HEADS], F32, kind="ExternalInput").ap()
    sstack = nc.dram_tensor("sstack", [8 * 2 * 128, HEADS * 128], F32, kind="ExternalInput").ap()
    gstin = nc.dram_tensor("gstin", [64, 64], F32, kind="ExternalInput").ap()
    dcsin = nc.dram_tensor("dcsin", [64, NC_], F32, kind="ExternalInput").ap()
    slocal = nc.dram_tensor("slocal", [2 * NC_ * 128, HEADS * 128], F32, kind="ExternalInput").ap()
    x2T = nc.dram_tensor("x2T", [D, tok], F32, kind="ExternalOutput").ap()

    opb = nc.dram_tensor("opb", [INTER, D], BF16).ap()
    ynd = nc.dram_tensor("ynd", [tok, INTER], BF16).ap()
    gl_d = nc.dram_tensor("gl_d", [64, 32], F32).ap()

    with contextlib.ExitStack() as es:
        sb = lambda n, sh, dt: es.enter_context(nc.sbuf_tensor(n, sh, dt))
        hin_s = sb("hin_s", [128, 64 * 128], F32)
        se_s = sb("se_s", [128, 2 * 128], F32)
        pl_s = sb("pl_s", [128, 2 * HEADS * 128], F32)
        pgb_s = sb("pgb_s", [128, 2 * HEADS * 128], BF16)
        gst_s = sb("gst_s", [64, 64], F32)
        gred_s = sb("gred_s", [64, 8], F32)
        gwork_s = sb("gwork_s", [64, 256], F32)
        dcs_s = sb("dcs_s", [64, NC_], F32)
        lam_s = sb("lam_s", [64, NC_ + 1], F32)
        zro_s = sb("zro_s", [64, NC_], F32)
        flat_s = sb("flat_s", [1, 1600], F32)
        gb_s = sb("gb_s", [128, 512], F32)
        lamb_s = sb("lamb_s", [128, 64 * (NC_ + 1)], F32)
        cf_s = sb("cf_s", [128, 2 * CHUNK], BF16)
        ydt_s = sb("ydt_s", [128, 2 * 512], BF16)
        xst_s = sb("xst_s", [128, 2 * 512], BF16)
        sgt_s = sb("sgt_s", [128, 2 * 512], BF16)
        sdt_s = sb("sdt_s", [128, 2 * 32], BF16)
        dbc_s = sb("dbc_s", [128, HEADS], F32)
        nwb_s = sb("nwb_s", [128, INTER], BF16)
        yb_s = sb("yb_s", [128, INTER], F32)
        t1_s = sb("t1_s", [128, 4 * 512], F32)
        sqr_s = sb("sqr_s", [128, 1], F32)
        inv_s = sb("inv_s", [128, 4], F32)
        eps_s = sb("eps_s", [128, 1], F32)
        ones_s = sb("ones_s", [1, 128], F32)
        yev_s = sb("yev_s", [128, 2 * INTER], BF16)
        wog_s = sb("wog_s", [128, 2 * 8 * 128], BF16)
        ymv_s = sb("ymv_s", [128, 2 * 512], BF16)
        xr_s = sb("xr_s", [128, 4 * 512], F32)
        o_s = sb("o_s", [128, 4 * 512], F32)

        ps = es.enter_context(nc.psum_tensor("ps", [128, 4096], F32))
        p = Prog(nc, es)

        t_eps = p.step("gpsimd", lambda e: e.memset(eps_s[:], float(EPS)))
        t_on = p.step("gpsimd", lambda e: e.memset(ones_s[:], 1.0))
        t_z = p.step("gpsimd", lambda e: e.memset(zro_s[:], 0.0))
        t_l1 = p.step("gpsimd", lambda e: e.memset(lam_s[:, 0:1], 1.0))
        twc = [p.dma("gpsimd", opb[c * (INTER // 8):(c + 1) * (INTER // 8), :],
                     opwT[c * (INTER // 8):(c + 1) * (INTER // 8), :]) for c in range(8)]
        t_g0 = p.dma("sync", gst_s[:], gstin)
        t_im = p.step("gpsimd", lambda e: e.iota(
            gwork_s[:, 0:64].rearrange("p (a b) -> p a b", b=8),
            pattern=[[0, 8], [1, 8]], base=0, channel_multiplier=0,
            allow_small_or_imprecise_dtypes=True))
        t_m1 = p.step("vector", lambda e: e.tensor_scalar(
            gwork_s[:, 64:128], gwork_s[:, 0:64], 1.0, None, ALU.is_ge),
            waits=[t_im], inc="DV")
        t_d0g = p.step("vector", lambda e: e.tensor_mul(
            gwork_s[:, 128:192], gst_s[:], gwork_s[:, 64:128]),
            waits=[t_m1, t_g0], inc="DV")
        t_d1g = p.step("vector", lambda e: e.tensor_sub(
            gwork_s[:, 192:256], gst_s[:], gwork_s[:, 128:192]),
            waits=[t_d0g], inc="DV")
        t_g1s = p.step("vector", lambda e: e.tensor_tensor_scan(
            gwork_s[:, 0:64], gwork_s[:, 128:192], gwork_s[:, 192:256], 0.0,
            ALU.mult, ALU.add), waits=[t_d1g], inc="DV")
        gprodap = bass.AP(gwork_s.ap().tensor, gwork_s.ap().offset + 7,
                          [list(gwork_s.ap().ap[0]), [8, 8]])
        t_g1 = p.step("vector", lambda e: e.tensor_copy(gred_s[:], gprodap),
                      waits=[t_g1s], inc="DV")
        t_d0 = p.dma("sync", dcs_s[:], dcsin)
        t_d1 = p.step("vector", lambda e: e.tensor_tensor_scan(
            lam_s[:, 1:NC_ + 1], dcs_s[:], zro_s[:], 1.0, ALU.mult, ALU.add),
            waits=[t_d0, t_z], inc="DV")
        t_f1a = p.dma("sync", gl_d[:, 0:8], gred_s[:], waits=[t_g1])
        t_f2a = p.dma("sync", gl_d[:, 8:8 + NC_ + 1], lam_s[:], waits=[t_d1, t_l1])
        t_f1 = p.dma("sync", flat_s[0:1, 0:512].rearrange("p (a b) -> p a b", a=64),
                     bass.AP(gl_d.tensor, 0, [[0, 1], [32, 64], [1, 8]]),
                     waits=[t_f1a])
        t_f2 = p.dma("sync", flat_s[0:1, 512:512 + 64 * (NC_ + 1)]
                     .rearrange("p (a b) -> p a b", a=64),
                     bass.AP(gl_d.tensor, 8, [[0, 1], [32, 64], [1, NC_ + 1]]),
                     waits=[t_f2a])
        BCW = 512 + 64 * (NC_ + 1)
        t_bc = None
        for w0 in range(0, BCW, 512):
            w1 = min(w0 + 512, BCW)
            t_bc = p.step("tensor", lambda e, w0=w0, w1=w1: e.matmul(
                ps[:, w0:w1], ones_s[:], flat_s[0:1, w0:w1], start=True, stop=True),
                waits=[t_f1, t_f2, t_on], inc="PE")
        t_gb = p.step("vector", lambda e: e.tensor_copy(gb_s[:], ps[:, 0:512]),
                      waits=[t_bc], inc="DV")
        t_lb = p.step("vector", lambda e: e.tensor_copy(
            lamb_s[:], ps[:, 512:512 + 64 * (NC_ + 1)]), waits=[t_bc], inc="DV")
        t_dv = p.dma("sync", flat_s[0:1, 512 + 64 * (NC_ + 1):512 + 64 * (NC_ + 1) + 32],
                     dvec, waits=[t_lb])
        t_db = p.step("tensor", lambda e: e.matmul(
            ps[:, 1024:1024 + 32], ones_s[:],
            flat_s[0:1, 512 + 64 * (NC_ + 1):512 + 64 * (NC_ + 1) + 32],
            start=True, stop=True), waits=[t_dv], inc="PE")
        t_db2 = p.step("vector", lambda e: e.tensor_copy(dbc_s[:], ps[:, 1024:1024 + 32]),
                       waits=[t_db], inc="DV")
        t_nw = p.dma("sync", flat_s[0:1, 0:512], normw[0:1, 0:512],
                     waits=[t_gb]) if False else None
        nw_toks = []
        for q in range(4):
            t_nw1 = p.dma("sync", flat_s[0:1, 0:1024], normw[0:1, q * 1024:(q + 1) * 1024],
                          waits=[t_gb, t_lb, t_db, t_db2] + nw_toks +
                                ([("PE", p.cnt["PE"])] if q == 0 else []))
            t_nw2a = p.step("tensor", lambda e, q=q: e.matmul(
                ps[:, 2048:2560], ones_s[:], flat_s[0:1, 0:512], start=True, stop=True),
                waits=[t_nw1], inc="PE")
            t_nw2 = p.step("tensor", lambda e, q=q: e.matmul(
                ps[:, 2560:3072], ones_s[:], flat_s[0:1, 512:1024], start=True, stop=True),
                waits=[t_nw1], inc="PE")
            t_nw3 = p.step("vector", lambda e, q=q: e.tensor_copy(
                nwb_s[:, q * 1024:(q + 1) * 1024], ps[:, 2048:3072]),
                waits=[t_nw2, t_nw2a], inc="DV")
            nw_toks = [t_nw3]
        t_nwd = nw_toks[0]
        t_h0a = p.dma("sync", hin_s[:, 0:HEADS * 128], sstack[0:128, :])
        t_h0 = p.dma("sync", hin_s[:, HEADS * 128:2 * HEADS * 128], sstack[128:256, :])
        fold_done = t_h0
        for en in range(1, 8):
            for bh in range(64):
                slot = (en * 64 + bh) % 2
                bb, hh2 = bh // 32, bh % 32
                t_se = p.dma("sync", se_s[:, slot * 128:(slot + 1) * 128],
                             bass.AP(sstack.tensor,
                                     ((en * 2 + bb) * 128) * (HEADS * 128) + hh2 * 128,
                                     [[HEADS * 128, 128], [1, 128]]),
                             waits=[fold_done])
                fold_done = p.step("vector", lambda e, bh=bh, en=en, slot=slot: e.scalar_tensor_tensor(
                    hin_s[:, bh * 128:(bh + 1) * 128], hin_s[:, bh * 128:(bh + 1) * 128],
                    gb_s[:, bh * 8 + en:bh * 8 + en + 1],
                    se_s[:, slot * 128:(slot + 1) * 128], ALU.mult, ALU.add),
                    waits=[t_se, t_gb, t_h0, t_h0a], inc="DV")
        yn_st = {}
        prev_pl = {}
        last_yn_store = None
        for b in range(2):
            co = b * TH
            for c in range(NC_):
                pslot = c % 2
                plb = pl_s[:, pslot * HEADS * 128:(pslot + 1) * HEADS * 128]
                t_pl = p.dma("sync", plb,
                             slocal[(b * NC_ + c) * 128:(b * NC_ + c) * 128 + 128, :],
                             waits=[prev_pl.get((b, c), fold_done)])
                pgb = pgb_s[:, pslot * HEADS * 128:(pslot + 1) * HEADS * 128]
                pg_done = []
                for h in range(HEADS):
                    col = (b * 32 + h) * (NC_ + 1) + c
                    pg_done.append(p.step("vector", lambda e, h=h, col=col, plb=plb, pgb=pgb, b=b: e.scalar_tensor_tensor(
                        pgb[:, h * 128:(h + 1) * 128], hin_s[:, (b * 32 + h) * 128:(b * 32 + h + 1) * 128],
                        lamb_s[:, col:col + 1], plb[:, h * 128:(h + 1) * 128],
                        ALU.mult, ALU.add),
                        waits=[t_pl, t_lb, fold_done], inc="DV"))
                prev_pl[(b, c + 2)] = pg_done[-1]
                t_cf = p.dma("sync", cf_s[:, 0:CHUNK], Cfm[:, co + c * CHUNK:co + (c + 1) * CHUNK],
                             waits=[("PE", p.cnt["PE"])])
                for lh in range(2):
                    tg = 2 * c + lh
                    t_sd = p.dma("sync", sdt_s[:, (tg % 2) * 32:(tg % 2) * 32 + 32],
                                 sdtm[co + tg * 128:co + (tg + 1) * 128, :],
                                 waits=[yn_st.get((b, tg - 2))])
                    yass = []
                    for hg in range(8):
                        yslot = hg % 2
                        t_yd = p.dma("scalar", ydt_s[:, yslot * 512:(yslot + 1) * 512],
                                     ydiag[co + tg * 128:co + (tg + 1) * 128, hg * 512:(hg + 1) * 512],
                                     waits=[yass[-2] if len(yass) >= 2 else yn_st.get((b, tg - 2))])
                        t_xs = p.dma("scalar", xst_s[:, yslot * 512:(yslot + 1) * 512],
                                     xstm[co + tg * 128:co + (tg + 1) * 128, hg * 512:(hg + 1) * 512],
                                     waits=[yass[-2] if len(yass) >= 2 else None])
                        t_sg = p.dma("scalar", sgt_s[:, yslot * 512:(yslot + 1) * 512],
                                     sgtm[co + tg * 128:co + (tg + 1) * 128, hg * 512:(hg + 1) * 512],
                                     waits=[yass[-2] if len(yass) >= 2 else None])
                        mmt = None
                        for hh in range(4):
                            h = hg * 4 + hh
                            mmt = p.step("tensor", lambda e, h=h, hh=hh, lh=lh, pgb=pgb: e.matmul(
                                ps[:, hh * 512:hh * 512 + 128],
                                cf_s[:, lh * 128:(lh + 1) * 128],
                                pgb[:, h * 128:(h + 1) * 128],
                                start=True, stop=True),
                                waits=([t_cf, pg_done[-1]] +
                                       ([yass[-1]] if yass else [("DV", p.cnt["DV"])])) if hh == 0 else [],
                                inc="PE" if hh == 3 else None)
                        sdap = bass.AP(sdt_s.ap().tensor,
                                       sdt_s.ap().offset + (tg % 2) * 32 + hg * 4,
                                       [list(sdt_s.ap().ap[0]), [1, 4], [0, 128]])
                        dap = bass.AP(dbc_s.ap().tensor, dbc_s.ap().offset + hg * 4,
                                      [list(dbc_s.ap().ap[0]), [1, 4], [0, 128]])
                        psv = bass.AP(ps.ap().tensor, ps.ap().offset,
                                      [list(ps.ap().ap[0]), [512, 4], [1, 128]])
                        t_a = p.step("vector", lambda e, sdap=sdap, psv=psv: e.tensor_tensor(
                            t1_s[:, 0:512].rearrange("p (h d) -> p h d", h=4),
                            psv, sdap, ALU.mult),
                            waits=[mmt, t_sd], inc="DV")
                        t_b2 = p.step("vector", lambda e, yslot=yslot: e.tensor_tensor(
                            t1_s[:, 0:512], t1_s[:, 0:512], ydt_s[:, yslot * 512:(yslot + 1) * 512],
                            ALU.add), waits=[t_a, t_yd], inc="DV")
                        t_c2 = p.step("vector", lambda e, dap=dap, yslot=yslot: e.tensor_tensor(
                            t1_s[:, 512:1024].rearrange("p (h d) -> p h d", h=4),
                            xst_s[:, yslot * 512:(yslot + 1) * 512].rearrange("p (h d) -> p h d", h=4),
                            dap, ALU.mult), waits=[t_xs, t_db2], inc="DV")
                        t_d2 = p.step("vector", lambda e: e.tensor_add(
                            t1_s[:, 0:512], t1_s[:, 0:512], t1_s[:, 512:1024]),
                            waits=[t_c2, t_b2], inc="DV")
                        t_e2 = p.step("vector", lambda e, hg=hg, yslot=yslot: e.tensor_mul(
                            yb_s[:, hg * 512:(hg + 1) * 512], t1_s[:, 0:512],
                            sgt_s[:, yslot * 512:(yslot + 1) * 512]),
                            waits=[t_d2, t_sg], inc="DV")
                        yass.append(t_e2)
                    t_sq = p.step("scalar", lambda e: e.activation(
                        out=ps[:, 3072:3072 + INTER] if False else yev_s[:, 0:INTER],
                        in_=yb_s[:], func=AF.Square), waits=[yass[-1]], inc="AC")
                    t_rd = p.step("vector", lambda e: e.tensor_reduce(
                        sqr_s[:], yev_s[:, 0:INTER], AX.X, ALU.add),
                        waits=[t_sq], inc="DV")
                    t_ln = p.step("scalar", lambda e: e.activation(
                        out=inv_s[:, 0:1], in_=sqr_s[:], func=AF.Ln, scale=1.0 / INTER,
                        bias=eps_s[:, 0:1]), waits=[t_rd, t_eps], inc="AC")
                    t_iv = p.step("scalar", lambda e: e.activation(
                        out=inv_s[:, 0:1], in_=inv_s[:, 0:1], func=AF.Exp, scale=-0.5),
                        waits=[t_ln], inc="AC")
                    t_y5 = p.step("vector", lambda e: e.tensor_scalar(
                        yb_s[:], yb_s[:], inv_s[:, 0:1], None, ALU.mult),
                        waits=[t_iv], inc="DV")
                    t_y6 = p.step("vector", lambda e: e.tensor_mul(
                        yev_s[:, INTER:2 * INTER], yb_s[:], nwb_s[:]),
                        waits=[t_y5, t_nwd, last_yn_store], inc="DV")
                    yn_st[(b, tg)] = p.dma("sync", ynd[co + tg * 128:co + (tg + 1) * 128, :],
                                           yev_s[:, INTER:2 * INTER], waits=[t_y6])
                    last_yn_store = yn_st[(b, tg)]
        NXO = 4
        all_yn = [yn_st[(b, tg)] for b in range(2) for tg in range(TC)]
        o_hist = []
        o_done2 = []
        prev_guard = None
        mm2_done = {}
        for b in range(2):
            co = b * TH
            for nt in range(TH // 512):
                for o2h in range(2):
                    mm_last = None
                    for k in range(32):
                        wslot = k % 2
                        wbuf = wog_s[:, wslot * 1024:(wslot + 1) * 1024]
                        yslot = k % 2
                        ybuf = ymv_s[:, yslot * 512:(yslot + 1) * 512]
                        wg = [mm2_done[(b, nt, o2h, k - 2)]] if k >= 2 else \
                             ([prev_guard] if prev_guard else [])
                        t_wl = p.dma("gpsimd", wbuf,
                                     opb[k * 128:(k + 1) * 128, o2h * 1024:(o2h + 1) * 1024],
                                     waits=twc + wg)
                        t_yl = p.dma("sync", ybuf,
                                     ynd[co + nt * 512:co + (nt + 1) * 512, k * 128:(k + 1) * 128],
                                     waits=wg + [all_yn[b * TC + 4 * nt + 3],
                                                 all_yn[b * TC + 4 * nt + 2],
                                                 all_yn[b * TC + 4 * nt + 1],
                                                 all_yn[b * TC + 4 * nt]],
                                     transpose=True)
                        for o2 in range(8):
                            mm_last = p.step("tensor", lambda e, o2=o2, k=k, wbuf=wbuf, ybuf=ybuf: e.matmul(
                                ps[:, o2 * 512:(o2 + 1) * 512],
                                wbuf[:, o2 * 128:(o2 + 1) * 128], ybuf,
                                start=(k == 0), stop=(k == 31)),
                                waits=([t_wl, t_yl] + (wg if o2 == 0 else [])) if o2 == 0 else [],
                                inc="PE" if o2 == 7 else None)
                        mm2_done[(b, nt, o2h, k)] = mm_last
                    last_ac = None
                    for o2 in range(8):
                        od = o2h * 8 + o2
                        j = len(o_hist)
                        xrb = xr_s[:, (j % NXO) * 512:(j % NXO) * 512 + 512]
                        orb = o_s[:, (j % NXO) * 512:(j % NXO) * 512 + 512]
                        t1b = t1_s[:, 1024 + (j % 2) * 512:1024 + (j % 2) * 512 + 512]
                        t_xr = p.dma("sync", xrb,
                                     x1T[od * 128:(od + 1) * 128, co + nt * 512:co + (nt + 1) * 512],
                                     waits=[o_hist[j - NXO]] if j >= NXO else [])
                        t_t1 = p.step("scalar", lambda e, o2=o2, t1b=t1b: e.activation(
                            out=t1b, in_=ps[:, o2 * 512:(o2 + 1) * 512], func=AF.Copy),
                            waits=[mm_last] + ([o_done2[j - 2]] if j >= 2 else []), inc="AC")
                        t_o = p.step("vector", lambda e, orb=orb, xrb=xrb, t1b=t1b: e.tensor_add(
                            orb, t1b, xrb),
                            waits=[t_t1, t_xr] + ([o_hist[j - NXO]] if j >= NXO else []),
                            inc="DV")
                        t_st = p.dma("sync",
                                     x2T[od * 128:(od + 1) * 128, co + nt * 512:co + (nt + 1) * 512],
                                     orb, waits=[t_o])
                        o_hist.append(t_st)
                        o_done2.append(t_o)
                        last_ac = t_t1
                    prev_guard = last_ac
        p.step("gpsimd", lambda e: e.nop(), waits=o_hist[-Prog.NSEM:])
        block = es.enter_context(nc.Block())
        p.realize(block)
    return nc



DL1 = 4096
RL1 = 128


def build_l1pre(tok=TOK):
    nc = bass.Bass("TRN2", target_bir_lowering=False)
    TH = tok // 2
    PADB = 512
    THP = TH + PADB
    NT = TH // 512
    KD = D // 128
    KU = DL1 // 128

    xT = nc.dram_tensor("xT", [D, 2 * THP], F32, kind="ExternalInput").ap()
    ipwT = nc.dram_tensor("ipwT", [D, 2 * DL1], F32, kind="ExternalInput").ap()
    xpwT = nc.dram_tensor("xpwT", [DL1, 160], F32, kind="ExternalInput").ap()
    dtwT = nc.dram_tensor("dtwT", [RL1, DL1], F32, kind="ExternalInput").ap()
    dtbr = nc.dram_tensor("dtbr", [128, KU], F32, kind="ExternalInput").ap()
    cwr = nc.dram_tensor("cwr", [128, KU * 4], F32, kind="ExternalInput").ap()
    cbr = nc.dram_tensor("cbr", [128, KU], F32, kind="ExternalInput").ap()

    ucb = nc.dram_tensor("ucb", [DL1, tok], BF16, kind="ExternalOutput").ap()
    dtfo = nc.dram_tensor("dtfo", [DL1, tok], F32, kind="ExternalOutput").ap()
    dub = nc.dram_tensor("dub", [DL1, tok], BF16, kind="ExternalOutput").ap()
    sgb = nc.dram_tensor("sgb", [DL1, tok], BF16, kind="ExternalOutput").ap()
    bcs = nc.dram_tensor("bcs", [32, tok], F32, kind="ExternalOutput").ap()

    ipb = nc.dram_tensor("ipb", [D, 2 * DL1], BF16).ap()
    upre = nc.dram_tensor("upre", [DL1, 2 * THP], BF16).ap()
    ucd = nc.dram_tensor("ucd", [DL1, tok], BF16).ap()

    with contextlib.ExitStack() as es:
        sb = lambda n, sh, dt: es.enter_context(nc.sbuf_tensor(n, sh, dt))
        xn_s = sb("xn_s", [128, KD * THP], BF16)
        x_s = sb("x_s", [128, 2 * 2048], F32)
        xb_s = sb("xb_s", [128, 2048], BF16)
        sq_s = sb("sq_s", [128, 2048], BF16)
        tmp_s = sb("tmp_s", [128, 2048], F32)
        ones_s = sb("ones_s", [128, 128], BF16)
        m_s = sb("m_s", [128, 2048], F32)
        inv_s = sb("inv_s", [128, 2048], F32)
        var_s = sb("var_s", [128, 2048], F32)
        wfg_s = sb("wfg_s", [128, 2 * KD * 128], BF16)
        ev_s = sb("ev_s", [128, 2 * THP], BF16)
        cw_s = sb("cw_s", [128, KU * 4], F32)
        cb_s = sb("cb_s", [128, KU], F32)
        dtb_s = sb("dtb_s", [128, KU], F32)
        hb_s = sb("hb_s", [128, 2 * THP], BF16)
        acc_s = sb("acc_s", [128, TH], F32)
        sx_s = sb("sx_s", [128, 2 * TH], BF16)
        xpg_s = sb("xpg_s", [128, 2 * 160], BF16)
        ssm_s = sb("ssm_s", [128, TH], BF16)
        bc_s = sb("bc_s", [32, TH], F32)
        dtw_s = sb("dtw_s", [128, 2 * 128], BF16)
        d1_s = sb("d1_s", [128, 512], F32)
        d2_s = sb("d2_s", [128, 512], F32)
        do_s = sb("do_s", [128, 2 * 512], F32)
        db_s = sb("db_s", [128, 2 * 512], BF16)
        ub_s = sb("ub_s", [128, 2 * 512], BF16)
        one_s = sb("one_s", [128, 1], F32)
        eps_s = sb("eps_s", [128, 1], F32)

        ps = es.enter_context(nc.psum_tensor("ps", [128, 4096], F32))
        p = Prog(nc, es)

        t_ones = p.step("gpsimd", lambda e: e.memset(ones_s[:], 1.0))
        t_eps = p.step("gpsimd", lambda e: e.memset(eps_s[:], float(EPS)))
        t_one = p.step("gpsimd", lambda e: e.memset(one_s[:], 1.0))
        t_cw = p.dma("sync", cw_s[:], cwr)
        t_cb = p.dma("sync", cb_s[:], cbr)
        t_dtbl = p.dma("sync", dtb_s[:], dtbr)
        twl = [p.dma("gpsimd", ipb[c * (D // 8):(c + 1) * (D // 8), :],
                     ipwT[c * (D // 8):(c + 1) * (D // 8), :]) for c in range(8)]

        guard = {"prev": []}

        def stats_xn(b, col0, W, m_b, inv_b, extra_guard):
            NTW = W // 512
            xo = col0 - b * THP
            xdma = [None] * KD
            stat_mm = [None] * KD
            xbw = [None] * KD
            sqw = [None] * KD
            for k in range(KD):
                xbuf = x_s[:, (k % 2) * 2048:(k % 2) * 2048 + W]
                xdma[k] = p.dma("sync", xbuf,
                                xT[k * 128:(k + 1) * 128, col0:col0 + W],
                                waits=([sqw[k - 2], xbw[k - 2]] if k >= 2 else list(extra_guard)))
                xbw[k] = p.step("scalar", lambda e, xb=xbuf, W=W: e.activation(
                    out=xb_s[:, 0:W], in_=xb, func=AF.Copy),
                    waits=[xdma[k], stat_mm[k - 1]], inc="AC")
                sqw[k] = p.step("scalar", lambda e, xb=xbuf, W=W: e.activation(
                    out=sq_s[:, 0:W], in_=xb, func=AF.Square),
                    waits=[xdma[k]], inc="AC")
                last = None
                for n in range(NTW):
                    p.step("tensor", lambda e, k=k, n=n: e.matmul(
                        ps[:, n * 512:(n + 1) * 512], ones_s[:],
                        xb_s[:, n * 512:(n + 1) * 512],
                        start=(k == 0), stop=(k == KD - 1)),
                        waits=([xbw[k], t_ones] + guard["prev"]) if k == 0 else [xbw[k]],
                        inc="PE")
                    last = p.step("tensor", lambda e, k=k, n=n: e.matmul(
                        ps[:, 2048 + n * 512:2048 + (n + 1) * 512], ones_s[:],
                        sq_s[:, n * 512:(n + 1) * 512],
                        start=(k == 0), stop=(k == KD - 1)),
                        waits=[sqw[k]], inc="PE")
                stat_mm[k] = last
            t_m = p.step("vector", lambda e, W=W, m_b=m_b: e.tensor_scalar(
                m_b[:, 0:W], ps[:, 0:W], 1.0 / D, None, ALU.mult),
                waits=[stat_mm[KD - 1]] + guard["prev"] + list(extra_guard), inc="DV")
            t_msq = p.step("scalar", lambda e, W=W, m_b=m_b: e.activation(
                out=var_s[:, 0:W], in_=m_b[:, 0:W], func=AF.Square),
                waits=[t_m], inc="AC")
            t_var = p.step("vector", lambda e, W=W: e.scalar_tensor_tensor(
                var_s[:, 0:W], ps[:, 2048:2048 + W], 1.0 / D, var_s[:, 0:W],
                ALU.mult, ALU.subtract), waits=[t_msq], inc="AC" if False else "DV")
            t_ln = p.step("scalar", lambda e, W=W, inv_b=inv_b: e.activation(
                out=inv_b[:, 0:W], in_=var_s[:, 0:W], func=AF.Ln, bias=eps_s[:, 0:1]),
                waits=[t_var, t_eps], inc="AC")
            t_inv = p.step("scalar", lambda e, W=W, inv_b=inv_b: e.activation(
                out=inv_b[:, 0:W], in_=inv_b[:, 0:W], func=AF.Exp, scale=-0.5),
                waits=[t_ln], inc="AC")
            xn_done = [None] * KD
            for k in range(KD):
                xbuf = x_s[:, (k % 2) * 2048:(k % 2) * 2048 + W]
                t_x = p.dma("sync", xbuf,
                            xT[k * 128:(k + 1) * 128, col0:col0 + W],
                            waits=[xn_done[k - 2]] if k >= 2 else [t_inv])
                t_sub = p.step("vector", lambda e, xb=xbuf, W=W, m_b=m_b: e.tensor_sub(
                    tmp_s[:, 0:W], xb, m_b[:, 0:W]),
                    waits=[t_x, t_inv] + ([xn_done[k - 1]] if k >= 1 else []), inc="DV")
                xn_done[k] = p.step("vector", lambda e, k=k, W=W, xo=xo, inv_b=inv_b: e.tensor_mul(
                    xn_s[:, k * THP + xo:k * THP + xo + W], tmp_s[:, 0:W], inv_b[:, 0:W]),
                    waits=[t_sub], inc="DV")
            guard["prev"] = [t_m, t_var]
            return xn_done[KD - 1], t_inv

        sg_done = {}
        prev_main_xn = None
        for b in range(2):
            co = b * THP
            oo = b * TH
            t_xnp, t_invp = stats_xn(b, co, PADB, m_s, inv_s,
                                     [prev_main_xn] if prev_main_xn else [])
            t_xn, t_inv = stats_xn(b, co + PADB, TH, m_s, inv_s, [t_xnp, t_invp])
            prev_main_xn = t_xn
            mm_done = [None] * (2 * KU)
            ev_done = [None] * (2 * KU)
            st_done = [None] * (2 * KU)
            for m in range(2 * KU):
                is_u = m < KU
                slot = m % 2
                wbuf = wfg_s[:, slot * KD * 128:(slot + 1) * KD * 128]
                src2 = bass.AP(ipb.tensor, m * 128,
                               [[2 * DL1, 128], [128 * 2 * DL1, KD], [1, 128]])
                t_w = p.dma("gpsimd", wbuf.rearrange("p (kc o) -> p kc o", kc=KD),
                            src2, waits=twl + ([mm_done[m - 2]] if m >= 2 else []))
                cols = list(range(0, THP, 512)) if is_u else list(range(PADB, THP, 512))
                mm_last = None
                for k in range(KD):
                    for ni, cx in enumerate(cols):
                        mm_last = p.step("tensor", lambda e, k=k, ni=ni, cx=cx, wbuf=wbuf: e.matmul(
                            ps[:, ((ni % 4) if len(cols) == 4 else (ni % 5 if False else ni % 4)) * 512 + (2048 if ni >= 4 else 0):
                               ((ni % 4) if len(cols) == 4 else ni % 4) * 512 + (2048 if ni >= 4 else 0) + 512]
                            if False else
                            (ps[:, ni * 512:(ni + 1) * 512] if ni < 4 else ps[:, 2048:2560]),
                            wbuf[:, k * 128:(k + 1) * 128],
                            xn_s[:, k * THP + cx:k * THP + cx + 512],
                            start=(k == 0), stop=(k == KD - 1)),
                            waits=([t_w, t_xn, t_xnp] +
                                   ([ev_done[m - 1]] if m >= 1 else []) + guard["prev"])
                            if (k == 0 and ni == 0) else [],
                            inc="PE" if (k == KD - 1 and ni == len(cols) - 1) else None)
                mm_done[m] = mm_last
                W1 = min(len(cols), 4) * 512
                W2 = (len(cols) - 4) * 512 if len(cols) > 4 else 0
                if is_u:
                    ebuf = ev_s[:, slot * THP:slot * THP + THP]
                    ev_done[m] = p.step("scalar", lambda e, ebuf=ebuf, W1=W1: e.activation(
                        out=ebuf[:, 0:W1], in_=ps[:, 0:W1], func=AF.Copy),
                        waits=[mm_last, st_done[m - 2]], inc="AC")
                    evl = [ev_done[m]]
                    if W2:
                        evl.append(p.step("scalar", lambda e, ebuf=ebuf, W1=W1, W2=W2: e.activation(
                            out=ebuf[:, W1:W1 + W2], in_=ps[:, 2048:2048 + W2], func=AF.Copy),
                            waits=[], inc="AC"))
                    st_done[m] = p.dma("sync", upre[m * 128:(m + 1) * 128, co:co + THP],
                                       ebuf, waits=evl)
                else:
                    mg = m - KU
                    ebuf = ev_s[:, slot * THP:slot * THP + TH]
                    ev_done[m] = p.step("scalar", lambda e, ebuf=ebuf, W1=W1: e.activation(
                        out=ebuf, in_=ps[:, 0:W1], func=AF.Silu),
                        waits=[mm_last, st_done[m - 2]], inc="AC")
                    st_done[m] = p.dma("sync", sgb[mg * 128:(mg + 1) * 128, oo:oo + TH],
                                       ebuf, waits=[ev_done[m]])
                    sg_done[(b, mg)] = st_done[m]
            conv_done = [None] * KU
            ust = {}
            for mu in range(KU):
                slot = mu % 2
                hbuf = hb_s[:, slot * THP:(slot + 1) * THP]
                t_h = p.dma("sync", hbuf, upre[mu * 128:(mu + 1) * 128, co:co + THP],
                            waits=[st_done[mu], conv_done[mu - 2]] if mu >= 2 else [st_done[mu]])
                t_c0 = p.step("vector", lambda e, hbuf=hbuf, mu=mu: e.tensor_scalar(
                    acc_s[:], hbuf[:, PADB - 3:PADB - 3 + TH], cw_s[:, mu * 4:mu * 4 + 1],
                    None, ALU.mult), waits=[t_h, t_cw], inc="DV")
                tj = t_c0
                for j in range(1, 4):
                    tj = p.step("vector", lambda e, hbuf=hbuf, mu=mu, j=j: e.scalar_tensor_tensor(
                        acc_s[:], hbuf[:, PADB - 3 + j:PADB - 3 + j + TH],
                        cw_s[:, mu * 4 + j:mu * 4 + j + 1], acc_s[:], ALU.mult, ALU.add),
                        waits=[tj], inc="DV")
                sxb = sx_s[:, slot * TH:(slot + 1) * TH]
                conv_done[mu] = p.step("scalar", lambda e, sxb=sxb, mu=mu: e.activation(
                    out=sxb, in_=acc_s[:], func=AF.Silu, bias=cb_s[:, mu:mu + 1]),
                    waits=[tj, t_cb] + ust.get((mu - 2, "both"), []), inc="AC")
                t_u1 = p.dma("sync", ucd[mu * 128:(mu + 1) * 128, oo:oo + TH], sxb,
                             waits=[conv_done[mu]])
                t_u2 = p.dma("sync", ucb[mu * 128:(mu + 1) * 128, oo:oo + TH], sxb,
                             waits=[conv_done[mu]])
                ust[mu] = t_u2
                ust[(mu, "both")] = [t_u1, t_u2]
            xp_mm = None
            for k in range(KU):
                xslot = k % 2
                t_xw = p.dma("gpsimd", xpg_s[:, xslot * 160:(xslot + 1) * 160],
                             xpwT[k * 128:(k + 1) * 128, :],
                             waits=[xp_mm] if k >= 2 else [])
                ubuf = hb_s[:, (k % 2) * THP:(k % 2) * THP + TH]
                t_ul = p.dma("sync", ubuf, ucd[k * 128:(k + 1) * 128, oo:oo + TH],
                             waits=[ust[k], xp_mm] if k >= 2 else [ust[k]])
                for n in range(NT):
                    p.step("tensor", lambda e, k=k, n=n, ubuf=ubuf, xslot=xslot: e.matmul(
                        ps[:, n * 512:(n + 1) * 512],
                        xpg_s[:, xslot * 160:xslot * 160 + 128],
                        ubuf[:, n * 512:(n + 1) * 512],
                        start=(k == 0), stop=(k == KU - 1)),
                        waits=([t_xw, t_ul, guard["prev"][0]] if (k == 0 and n == 0) else [t_xw, t_ul]),
                        inc="PE")
                    xp_mm = p.step("tensor", lambda e, k=k, n=n, ubuf=ubuf, xslot=xslot: e.matmul(
                        ps[0:32, 2048 + n * 512:2048 + (n + 1) * 512],
                        xpg_s[:, xslot * 160 + 128:xslot * 160 + 160],
                        ubuf[:, n * 512:(n + 1) * 512],
                        start=(k == 0), stop=(k == KU - 1)),
                        waits=[], inc="PE")
            t_tse = p.step("scalar", lambda e: e.activation(
                out=ssm_s[:, 0:TH], in_=ps[:, 0:TH], func=AF.Copy),
                waits=[xp_mm], inc="AC")
            t_bce = p.step("vector", lambda e: e.tensor_copy(bc_s[:], ps[0:32, 2048:2048 + TH]),
                           waits=[xp_mm], inc="DV")
            t_bcs = p.dma("sync", bcs[:, oo:oo + TH], bc_s[:], waits=[t_bce])
            guard["prev"] = [t_tse, t_bce]
            t_dw = {}
            sp_last = {}
            dustore = []
            prev_tdt = None
            for m in range(KU):
                wslot = m % 2
                t_dwl = p.dma("gpsimd", dtw_s[:, wslot * 128:(wslot + 1) * 128],
                              dtwT[:, m * 128:(m + 1) * 128],
                              waits=[sp_last.get(m - 2)])
                for n in range(NT):
                    pslot = (m * NT + n) % 2
                    t_mm = p.step("tensor", lambda e, m=m, n=n, wslot=wslot, pslot=pslot: e.matmul(
                        ps[:, 3072 + pslot * 512:3072 + (pslot + 1) * 512],
                        dtw_s[:, wslot * 128:(wslot + 1) * 128],
                        ssm_s[:, n * 512:(n + 1) * 512],
                        start=True, stop=True),
                        waits=[t_dwl, t_tse, sp_last.get((m, n - 2) if n >= 2 else (m - 1, NT - 2 + n))],
                        inc="PE")
                    t_a = p.step("scalar", lambda e, m=m, pslot=pslot: e.activation(
                        out=d1_s[:], in_=ps[:, 3072 + pslot * 512:3072 + (pslot + 1) * 512],
                        func=AF.Identity, bias=dtb_s[:, m:m + 1]),
                        waits=[t_mm, t_dtbl, prev_tdt], inc="AC")
                    t_ab = p.step("scalar", lambda e: e.activation(
                        out=d2_s[:], in_=d1_s[:], func=AF.Abs), waits=[t_a], inc="AC")
                    t_e = p.step("scalar", lambda e: e.activation(
                        out=d2_s[:], in_=d2_s[:], func=AF.Exp, scale=-1.0), waits=[t_ab], inc="AC")
                    t_l = p.step("scalar", lambda e: e.activation(
                        out=d2_s[:], in_=d2_s[:], func=AF.Ln, bias=one_s[:, 0:1]),
                        waits=[t_e, t_one], inc="AC")
                    t_r = p.step("scalar", lambda e: e.activation(
                        out=d1_s[:], in_=d1_s[:], func=AF.Relu), waits=[t_l], inc="AC")
                    doslot = (m * NT + n) % 2
                    t_dt = p.step("vector", lambda e, doslot=doslot: e.tensor_add(
                        do_s[:, doslot * 512:(doslot + 1) * 512], d2_s[:], d1_s[:]),
                        waits=[t_r] + ([dustore[-2][0], dustore[-2][2]] if len(dustore) >= 2 else []),
                        inc="DV")
                    prev_tdt = t_dt
                    t_dts = p.dma("sync", dtfo[m * 128:(m + 1) * 128, oo + n * 512:oo + (n + 1) * 512],
                                  do_s[:, doslot * 512:(doslot + 1) * 512], waits=[t_dt])
                    t_ub = p.dma("scalar", ub_s[:, doslot * 512:(doslot + 1) * 512],
                                 ucd[m * 128:(m + 1) * 128, oo + n * 512:oo + (n + 1) * 512],
                                 waits=[ust[m], ust[(m, "both")][0]] +
                                       ([dustore[-2][1]] if len(dustore) >= 2 else []))
                    t_du = p.step("vector", lambda e, doslot=doslot: e.tensor_mul(
                        db_s[:, doslot * 512:(doslot + 1) * 512],
                        do_s[:, doslot * 512:(doslot + 1) * 512],
                        ub_s[:, doslot * 512:(doslot + 1) * 512]),
                        waits=[t_dt, t_ub], inc="DV")
                    t_dus = p.dma("sync", dub[m * 128:(m + 1) * 128, oo + n * 512:oo + (n + 1) * 512],
                                  db_s[:, doslot * 512:(doslot + 1) * 512], waits=[t_du])
                    sp_last[(m, n)] = t_dus
                    dustore.append((t_dus, t_du, t_dts))
                sp_last[m] = sp_last[(m, NT - 1)]
            guard["prev"] = [("PE", p.cnt["PE"])]
        p.step("gpsimd", lambda e: e.nop(),
               waits=[sp_last[KU - 1], t_bcs] + [sg_done[(bb, KU - 1)] for bb in range(2)])
        block = es.enter_context(nc.Block())
        p.realize(block)
    return nc



def build_scan(S=S, DSL=D // NCORES // 4 * 4 if False else 512, TCH=512):
    nc = bass.Bass("TRN2", target_bir_lowering=False)
    NG = DSL // 8
    NTC = S // TCH

    dtc = nc.dram_tensor("dtc", [2 * DSL, S], F32, kind="ExternalInput").ap()
    duc = nc.dram_tensor("duc", [2 * DSL, S], BF16, kind="ExternalInput").ap()
    uc = nc.dram_tensor("uc", [2 * DSL, S], BF16, kind="ExternalInput").ap()
    sgc = nc.dram_tensor("sgc", [2 * DSL, S], BF16, kind="ExternalInput").ap()
    alogr = nc.dram_tensor("alogr", [128, NG], F32, kind="ExternalInput").ap()
    dparr = nc.dram_tensor("dparr", [128, DSL // 128], F32, kind="ExternalInput").ap()
    bcf = nc.dram_tensor("bcf", [2 * 32, S], F32, kind="ExternalInput").ap()
    zc = nc.dram_tensor("zc", [2 * DSL, S], BF16, kind="ExternalOutput").ap()

    with contextlib.ExitStack() as es:
        sb = lambda n, sh, dt: es.enter_context(nc.sbuf_tensor(n, sh, dt))
        eg_s = sb("eg_s", [128, NG * 128], BF16)
        egl_s = sb("egl_s", [128, NG * 128], F32)
        sg2_s = sb("sg2_s", [128, NG * 128], BF16)
        eb_s = sb("eb_s", [16, 128], F32)
        A_s = sb("A_s", [128, NG], F32)
        dpar_s = sb("dpar_s", [128, DSL // 128], F32)
        io_s = sb("io_s", [128, NG * 128 // 64 if False else 8192], F32)
        dt_s = sb("dt_s", [128, 2 * TCH], F32)
        dth_s = sb("dth_s", [128, 2 * TCH], BF16)
        dtl_s = sb("dtl_s", [128, 2 * TCH], BF16)
        du_s = sb("du_s", [128, 2 * TCH], BF16)
        bcb_s = sb("bcb_s", [128, 2 * TCH], BF16)
        bcr_s = sb("bcr_s", [16, 2 * TCH], F32)
        da_s = sb("da_s", [128, 2 * TCH], F32)
        w_s = sb("w_s", [128, 2 * TCH], BF16)
        h_s = sb("h_s", [128, 2 * TCH], F32)
        hc_s = sb("hc_s", [128, 2 * TCH], BF16)
        hcar_s = sb("hcar_s", [128, NG], F32)
        u_s = sb("u_s", [128, 2 * TCH], BF16)
        sg_s = sb("sg_s", [128, 2 * TCH], BF16)
        z1_s = sb("z1_s", [128, TCH], F32)
        z2_s = sb("z2_s", [128, 2 * TCH], BF16)
        zr_s = sb("zr_s", [128, 1], F32)

        ps = es.enter_context(nc.psum_tensor("ps", [128, 4096], F32))
        p = Prog(nc, es)

        t_i0 = p.step("gpsimd", lambda e: e.iota(
            io_s[:, 0:1], pattern=[[0, 1]], base=0, channel_multiplier=1,
            allow_small_or_imprecise_dtypes=True))
        t_im = p.step("gpsimd", lambda e: e.memset(io_s[:, 3:4], 0.0))
        t_i1 = t_im
        for k in range(1, 8):
            t_ge = p.step("vector", lambda e, k=k: e.tensor_scalar(
                io_s[:, 1:2], io_s[:, 0:1], float(16 * k), None, ALU.is_ge),
                waits=[t_i0], inc="DV")
            t_i1 = p.step("vector", lambda e: e.tensor_add(
                io_s[:, 3:4], io_s[:, 3:4], io_s[:, 1:2]),
                waits=[t_ge, t_im], inc="DV")
        tegs = []
        for g in range(NG):
            t_i2 = p.step("gpsimd", lambda e, g=g: e.iota(
                io_s[:, 128:256], pattern=[[1, 128]], base=-(g * 8), channel_multiplier=0,
                allow_small_or_imprecise_dtypes=True), waits=[t_i1])
            tegs.append(t_i2)
        t_e = None
        prev_a2 = None
        prev_e = None
        for g in range(NG):
            t_a1 = p.step("gpsimd", lambda e, g=g: e.iota(
                io_s[:, 256:384].rearrange("p (a b) -> p a b", b=16),
                pattern=[[1, 8], [0, 16]], base=(g % 16) * 8, channel_multiplier=-1,
                allow_small_or_imprecise_dtypes=True), waits=[prev_a2])
            t_a2 = p.step("vector", lambda e, g=g: e.tensor_scalar(
                eg_s[:, g * 128:(g + 1) * 128], io_s[:, 256:384], 0.0, None,
                ALU.is_equal), waits=[t_a1], inc="DV")
            t_a3 = p.step("vector", lambda e, g=g: e.tensor_copy(
                egl_s[:, g * 128:(g + 1) * 128], eg_s[:, g * 128:(g + 1) * 128]),
                waits=[t_a2], inc="DV")
            t_a4 = p.step("gpsimd", lambda e, g=g: e.iota(
                io_s[:, 384:512], pattern=[[1, 128]],
                base=-((g * 8) % 128), channel_multiplier=0,
                allow_small_or_imprecise_dtypes=True), waits=[prev_e])
            t_a5 = p.step("vector", lambda e, g=g: e.tensor_scalar(
                io_s[:, 512:640], io_s[:, 384:512], io_s[:, 3:4], None, ALU.subtract),
                waits=[t_a4, t_i1], inc="DV")
            t_e = p.step("vector", lambda e, g=g: e.tensor_scalar(
                sg2_s[:, g * 128:(g + 1) * 128], io_s[:, 512:640], 0.0, None,
                ALU.is_equal), waits=[t_a5], inc="DV")
            prev_a2 = t_a2
            prev_e = t_e
        t_b1 = p.step("gpsimd", lambda e: e.iota(
            io_s[0:16, 640:768], pattern=[[1, 128]], base=0, channel_multiplier=-1,
            allow_small_or_imprecise_dtypes=True))
        t_b2 = p.step("gpsimd", lambda e: e.iota(
            io_s[0:16, 768:896].rearrange("p (a b) -> p a b", b=16),
            pattern=[[0, 8], [1, 16]], base=0, channel_multiplier=-1,
            allow_small_or_imprecise_dtypes=True))
        t_b3 = p.step("vector", lambda e: e.tensor_scalar(
            eb_s[:], io_s[0:16, 768:896], 0.0, None, ALU.is_equal),
            waits=[t_b2], inc="DV")
        t_al = p.dma("sync", A_s[:], alogr)
        t_a6 = p.step("scalar", lambda e: e.activation(
            out=A_s[:], in_=A_s[:], func=AF.Exp), waits=[t_al], inc="AC")
        t_a7 = p.step("vector", lambda e: e.tensor_scalar(
            A_s[:], A_s[:], -1.0, None, ALU.mult), waits=[t_a6], inc="DV")
        t_dp = p.dma("sync", dpar_s[:], dparr)
        t_hz = p.step("gpsimd", lambda e: e.memset(hcar_s[:], 0.0))

        zst = {}
        for b in range(2):
            ro = b * DSL
            t_hz2 = p.step("gpsimd", lambda e: e.memset(hcar_s[:], 0.0),
                           waits=[("PE", p.cnt["PE"]), ("DV", p.cnt["DV"])] if b else [],
                           inc="GP")
            for tc in range(NTC):
                t0 = tc * TCH
                t_bc0 = p.dma("sync", bcr_s[:, 0:TCH], bcf[b * 32:b * 32 + 16, t0:t0 + TCH],
                              waits=[("DV", p.cnt["DV"]), ("GP", p.cnt["GP"])] if tc == 0 and b == 0 else
                              [("PE", p.cnt["PE"])])
                t_bc0b = p.dma("sync", bcr_s[:, TCH:2 * TCH], bcf[b * 32 + 16:b * 32 + 32, t0:t0 + TCH],
                               waits=[])
                t_bm1 = p.step("tensor", lambda e: e.matmul(
                    ps[:, 2048:2048 + TCH], eb_s[:], bcr_s[:, 0:TCH],
                    start=True, stop=True), waits=[t_bc0, t_b3], inc="PE")
                t_bm2 = p.step("tensor", lambda e: e.matmul(
                    ps[:, 2560:2560 + TCH], eb_s[:], bcr_s[:, TCH:2 * TCH],
                    start=True, stop=True), waits=[t_bc0b], inc="PE")
                t_be1 = p.step("vector", lambda e: e.tensor_copy(
                    bcb_s[:, 0:TCH], ps[:, 2048:2048 + TCH]), waits=[t_bm1], inc="DV")
                t_be2 = p.step("vector", lambda e: e.tensor_copy(
                    bcb_s[:, TCH:2 * TCH], ps[:, 2560:2560 + TCH]), waits=[t_bm2], inc="DV")
                gdone = {}
                for gb in range(4):
                    dr = ro + gb * 128
                    dslot = gb % 2
                    t_dtl = p.dma("sync", dt_s[:, dslot * TCH:(dslot + 1) * TCH],
                                  dtc[dr:dr + 128, t0:t0 + TCH],
                                  waits=[gdone.get(gb - 2, ("PE", p.cnt["PE"]) if (tc or b) and gb < 2 else None)])
                    t_dul = p.dma("scalar", du_s[:, dslot * TCH:(dslot + 1) * TCH],
                                  duc[dr:dr + 128, t0:t0 + TCH],
                                  waits=[gdone.get(gb - 2)])
                    t_hi = p.step("scalar", lambda e, dslot=dslot: e.activation(
                        out=dth_s[:, dslot * TCH:(dslot + 1) * TCH],
                        in_=dt_s[:, dslot * TCH:(dslot + 1) * TCH], func=AF.Copy),
                        waits=[t_dtl, gdone.get(gb - 2)], inc="AC")
                    t_lo = p.step("vector", lambda e, dslot=dslot: e.tensor_sub(
                        dtl_s[:, dslot * TCH:(dslot + 1) * TCH].bitcast(F32) if False else
                        dt_s[:, dslot * TCH:(dslot + 1) * TCH],
                        dt_s[:, dslot * TCH:(dslot + 1) * TCH],
                        dth_s[:, dslot * TCH:(dslot + 1) * TCH]),
                        waits=[t_hi], inc="DV")
                    t_lo2 = p.step("scalar", lambda e, dslot=dslot: e.activation(
                        out=dtl_s[:, dslot * TCH:(dslot + 1) * TCH],
                        in_=dt_s[:, dslot * TCH:(dslot + 1) * TCH], func=AF.Copy),
                        waits=[t_lo], inc="AC")
                    yev = None
                    for gi in range(16):
                        g = gb * 16 + gi
                        hslot = g % 2
                        pso = 3072 + (g % 2) * 512
                        m1 = p.step("tensor", lambda e, g=g, pso=pso, dslot=dslot: e.matmul(
                            ps[:, pso:pso + TCH], eg_s[:, g * 128:(g + 1) * 128],
                            dth_s[:, dslot * TCH:(dslot + 1) * TCH],
                            start=True, stop=False),
                            waits=[t_hi, t_lo2, ("AC", p.cnt["AC"])], inc=None)
                        m2 = p.step("tensor", lambda e, g=g, pso=pso, dslot=dslot: e.matmul(
                            ps[:, pso:pso + TCH], eg_s[:, g * 128:(g + 1) * 128],
                            dtl_s[:, dslot * TCH:(dslot + 1) * TCH],
                            start=False, stop=True), inc="PE")
                        t_da = p.step("scalar", lambda e, g=g, pso=pso, hslot=hslot: e.activation(
                            out=da_s[:, hslot * TCH:(hslot + 1) * TCH],
                            in_=ps[:, pso:pso + TCH], func=AF.Exp,
                            scale=A_s[:, g:g + 1]),
                            waits=[m2, t_a7], inc="AC")
                        m3 = p.step("tensor", lambda e, g=g, pso=pso, dslot=dslot: e.matmul(
                            ps[:, pso:pso + TCH], eg_s[:, g * 128:(g + 1) * 128],
                            du_s[:, dslot * TCH:(dslot + 1) * TCH],
                            start=True, stop=True),
                            waits=[t_dul, t_da], inc="PE")
                        t_w = p.step("vector", lambda e, pso=pso, hslot=hslot: e.tensor_tensor(
                            w_s[:, hslot * TCH:(hslot + 1) * TCH],
                            ps[:, pso:pso + TCH], bcb_s[:, 0:TCH], ALU.mult),
                            waits=[m3, t_be1], inc="DV")
                        t_sc = p.step("vector", lambda e, g=g, hslot=hslot: e.tensor_tensor_scan(
                            h_s[:, hslot * TCH:(hslot + 1) * TCH],
                            da_s[:, hslot * TCH:(hslot + 1) * TCH],
                            w_s[:, hslot * TCH:(hslot + 1) * TCH],
                            hcar_s[:, g:g + 1], ALU.mult, ALU.add),
                            waits=[t_w, t_da, t_hz2], inc="DV")
                        t_cr = p.step("vector", lambda e, g=g, hslot=hslot: e.tensor_copy(
                            hcar_s[:, g:g + 1], h_s[:, hslot * TCH + TCH - 1:hslot * TCH + TCH]),
                            waits=[t_sc], inc="DV")
                        t_hc = p.step("gpsimd", lambda e, hslot=hslot: e.tensor_tensor(
                            hc_s[:, hslot * TCH:(hslot + 1) * TCH],
                            h_s[:, hslot * TCH:(hslot + 1) * TCH],
                            bcb_s[:, TCH:2 * TCH], ALU.mult),
                            waits=[t_sc, t_be2], inc="GP")
                        yev = p.step("tensor", lambda e, g=g, gb=gb, gi=gi, hslot=hslot: e.matmul(
                            ps[:, gb * TCH:(gb + 1) * TCH],
                            sg2_s[:, g * 128:(g + 1) * 128],
                            hc_s[:, hslot * TCH:(hslot + 1) * TCH],
                            start=(gi == 0), stop=(gi == 15)),
                            waits=[t_hc] + ([zst.get((b, tc - 1, gb))] if gi == 0 else []),
                            inc="PE" if gi == 15 else None)
                    gdone[gb] = yev
                z2toks = {}
                for gb in range(4):
                    dr = ro + gb * 128
                    zslot = gb % 2
                    zguard = [zst.get((b, tc - 1, gb)), z2toks.get(gb - 2)]
                    t_ul = p.dma("sync", u_s[:, zslot * TCH:(zslot + 1) * TCH],
                                 uc[dr:dr + 128, t0:t0 + TCH], waits=zguard)
                    t_sgl = p.dma("scalar", sg_s[:, zslot * TCH:(zslot + 1) * TCH],
                                  sgc[dr:dr + 128, t0:t0 + TCH], waits=zguard)
                    t_z1 = p.step("vector", lambda e, gb=gb, zslot=zslot: e.scalar_tensor_tensor(
                        z1_s[:], u_s[:, zslot * TCH:(zslot + 1) * TCH],
                        dpar_s[:, gb:gb + 1], ps[:, gb * TCH:(gb + 1) * TCH],
                        ALU.mult, ALU.add),
                        waits=[t_ul, gdone[gb], t_dp], inc="DV")
                    t_z2 = p.step("vector", lambda e, gb=gb, zslot=zslot: e.tensor_mul(
                        z2_s[:, zslot * TCH:(zslot + 1) * TCH], z1_s[:],
                        sg_s[:, zslot * TCH:(zslot + 1) * TCH]),
                        waits=[t_z1, t_sgl, zst.get((b, tc, gb - 2), zst.get((b, tc - 1, gb - 2 + 4)))],
                        inc="DV")
                    z2toks[gb] = t_z2
                    zst[(b, tc, gb)] = p.dma("sync", zc[dr:dr + 128, t0:t0 + TCH],
                                             z2_s[:, zslot * TCH:(zslot + 1) * TCH],
                                             waits=[t_z2])
        p.step("gpsimd", lambda e: e.nop(),
               waits=[zst[(1, NTC - 1, gb)] for gb in range(4)] +
                     [zst[(0, NTC - 1, gb)] for gb in range(4)])
        block = es.enter_context(nc.Block())
        p.realize(block)
    return nc



def build_l1tail(tok=TOK):
    nc = bass.Bass("TRN2", target_bir_lowering=False)
    zT = nc.dram_tensor("zT", [DL1, tok], BF16, kind="ExternalInput").ap()
    x4T = nc.dram_tensor("x4T", [D, tok], F32, kind="ExternalInput").ap()
    opwT = nc.dram_tensor("opwT", [DL1, D], F32, kind="ExternalInput").ap()
    x5T = nc.dram_tensor("x5T", [D, tok], F32, kind="ExternalOutput").ap()
    opb = nc.dram_tensor("opb", [DL1, D], BF16).ap()

    NTT = tok // 512
    with contextlib.ExitStack() as es:
        sb = lambda n, sh, dt: es.enter_context(nc.sbuf_tensor(n, sh, dt))
        wog_s = sb("wog_s", [128, 2 * 8 * 128], BF16)
        ymv_s = sb("ymv_s", [128, 2 * 512], BF16)
        t1_s = sb("t1_s", [128, 2 * 512], F32)
        xr_s = sb("xr_s", [128, 4 * 512], F32)
        o_s = sb("o_s", [128, 4 * 512], F32)
        ps = es.enter_context(nc.psum_tensor("ps", [128, 4096], F32))
        p = Prog(nc, es)
        twc = [p.dma("gpsimd", opb[c * (DL1 // 8):(c + 1) * (DL1 // 8), :],
                     opwT[c * (DL1 // 8):(c + 1) * (DL1 // 8), :]) for c in range(8)]
        NXO = 4
        o_hist = []
        o_done2 = []
        prev_guard = None
        mmd = {}
        for nt in range(NTT):
            for o2h in range(2):
                mm_last = None
                for k in range(32):
                    wslot = k % 2
                    wbuf = wog_s[:, wslot * 1024:(wslot + 1) * 1024]
                    ybuf = ymv_s[:, (k % 2) * 512:(k % 2) * 512 + 512]
                    wg = [mmd[(nt, o2h, k - 2)]] if k >= 2 else \
                         ([prev_guard] if prev_guard else [])
                    t_wl = p.dma("gpsimd", wbuf,
                                 opb[k * 128:(k + 1) * 128, o2h * 1024:(o2h + 1) * 1024],
                                 waits=twc + wg)
                    t_yl = p.dma("sync", ybuf,
                                 zT[k * 128:(k + 1) * 128, nt * 512:(nt + 1) * 512],
                                 waits=wg)
                    for o2 in range(8):
                        mm_last = p.step("tensor", lambda e, o2=o2, k=k, wbuf=wbuf, ybuf=ybuf: e.matmul(
                            ps[:, o2 * 512:(o2 + 1) * 512],
                            wbuf[:, o2 * 128:(o2 + 1) * 128], ybuf,
                            start=(k == 0), stop=(k == 31)),
                            waits=([t_wl, t_yl] if o2 == 0 else []),
                            inc="PE" if o2 == 7 else None)
                    mmd[(nt, o2h, k)] = mm_last
                last_ac = None
                for o2 in range(8):
                    od = o2h * 8 + o2
                    j = len(o_hist)
                    xrb = xr_s[:, (j % NXO) * 512:(j % NXO) * 512 + 512]
                    orb = o_s[:, (j % NXO) * 512:(j % NXO) * 512 + 512]
                    t1b = t1_s[:, (j % 2) * 512:(j % 2) * 512 + 512]
                    t_xr = p.dma("sync", xrb,
                                 x4T[od * 128:(od + 1) * 128, nt * 512:(nt + 1) * 512],
                                 waits=[o_hist[j - NXO]] if j >= NXO else [])
                    t_t1 = p.step("scalar", lambda e, o2=o2, t1b=t1b: e.activation(
                        out=t1b, in_=ps[:, o2 * 512:(o2 + 1) * 512], func=AF.Copy),
                        waits=[mm_last] + ([o_done2[j - 2]] if j >= 2 else []), inc="AC")
                    t_o = p.step("vector", lambda e, orb=orb, xrb=xrb, t1b=t1b: e.tensor_add(
                        orb, t1b, xrb),
                        waits=[t_t1, t_xr] + ([o_hist[j - NXO]] if j >= NXO else []),
                        inc="DV")
                    t_st = p.dma("sync",
                                 x5T[od * 128:(od + 1) * 128, nt * 512:(nt + 1) * 512],
                                 orb, waits=[t_o])
                    o_hist.append(t_st)
                    o_done2.append(t_o)
                    last_ac = t_t1
                prev_guard = last_ac
        p.step("gpsimd", lambda e: e.nop(), waits=o_hist[-Prog.NSEM:])
        block = es.enter_context(nc.Block())
        p.realize(block)
    return nc



def build_gffn(H=GH, tok=TOK, d=D):
    nc = bass.Bass("TRN2", target_bir_lowering=False)
    KD = d // 128
    KH = H // 128
    TH = tok // 2
    NT = TH // 512

    xT = nc.dram_tensor("xT", [d, tok], F32, kind="ExternalInput").ap()
    wgTb = nc.dram_tensor("wgTb", [d, H], BF16, kind="ExternalInput").ap()
    wuTb = nc.dram_tensor("wuTb", [d, H], BF16, kind="ExternalInput").ap()
    wdTb = nc.dram_tensor("wdTb", [H, d], BF16, kind="ExternalInput").ap()
    x1T = nc.dram_tensor("x1T", [d, tok], F32, kind="ExternalOutput").ap()
    hbuf = nc.dram_tensor("hbuf", [H, TH], BF16).ap()

    with contextlib.ExitStack() as es:
        sb = lambda n, sh, dt: es.enter_context(nc.sbuf_tensor(n, sh, dt))
        xn_s = sb("xn_s", [128, KD * TH], BF16)
        x_s = sb("x_s", [128, 2 * TH], F32)
        xb_s = sb("xb_s", [128, TH], BF16)
        sq_s = sb("sq_s", [128, TH], BF16)
        tmp_s = sb("tmp_s", [128, TH], F32)
        ones_s = sb("ones_s", [128, 128], BF16)
        m_s = sb("m_s", [128, TH], F32)
        inv_s = sb("inv_s", [128, TH], F32)
        var_s = sb("var_s", [128, TH], F32)
        sg_s = sb("sg_s", [128, TH], F32)
        wgg_s = sb("wgg_s", [128, 2 * KD * 128], BF16)
        wug_s = sb("wug_s", [128, 2 * KD * 128], BF16)
        w1g_s = sb("w1g_s", [128, 2 * 2 * 128], BF16)
        h_s = sb("h_s", [128, 2 * TH], BF16)
        ev_s = sb("ev_s", [128, TH], BF16)
        t1_s = sb("t1_s", [128, 2 * 512], F32)
        xr_s = sb("xr_s", [128, 4 * 512], F32)
        o_s = sb("o_s", [128, 4 * 512], F32)
        eps_s = sb("eps_s", [128, 1], F32)

        ps = es.enter_context(nc.psum_tensor("ps", [128, 4096], F32))
        p = Prog(nc, es)

        t_ones = p.step("gpsimd", lambda e: e.memset(ones_s[:], 1.0))
        t_eps = p.step("gpsimd", lambda e: e.memset(eps_s[:], float(EPS)))

        phase_guards = []
        xn_last_tok = None
        inv_tok = None
        for th in range(2):
            co = th * TH
            prev_stores = phase_guards[-1][-Prog.NSEM:] if phase_guards else []
            prev_xn_last = xn_last_tok
            prev_inv_tok = inv_tok
            xdma = [None] * KD
            stat_mm = [None] * KD
            xbw = [None] * KD
            sqw = [None] * KD
            for k in range(KD):
                xbuf = x_s[:, (k % 2) * TH:(k % 2) * TH + TH]
                xdma[k] = p.dma("sync", xbuf,
                                xT[k * 128:(k + 1) * 128, co:co + TH],
                                waits=[sqw[k - 2], xbw[k - 2]] if k >= 2 else [])
                xbw[k] = p.step("scalar", lambda e, xb=xbuf: e.activation(
                    out=xb_s[:], in_=xb, func=AF.Copy),
                    waits=[xdma[k], stat_mm[k - 1]], inc="AC")
                sqw[k] = p.step("scalar", lambda e, xb=xbuf: e.activation(
                    out=sq_s[:], in_=xb, func=AF.Square),
                    waits=[xdma[k]], inc="AC")
                last = None
                for n in range(NT):
                    p.step("tensor", lambda e, k=k, n=n: e.matmul(
                        ps[:, n * 512:(n + 1) * 512], ones_s[:],
                        xb_s[:, n * 512:(n + 1) * 512],
                        start=(k == 0), stop=(k == KD - 1)),
                        waits=([xbw[k], t_ones] + prev_stores) if k == 0 else [xbw[k]],
                        inc="PE")
                    last = p.step("tensor", lambda e, k=k, n=n: e.matmul(
                        ps[:, 2048 + n * 512:2048 + (n + 1) * 512], ones_s[:],
                        sq_s[:, n * 512:(n + 1) * 512],
                        start=(k == 0), stop=(k == KD - 1)),
                        waits=[sqw[k]], inc="PE")
                stat_mm[k] = last
            t_m = p.step("vector", lambda e: e.tensor_scalar(
                m_s[:], ps[:, 0:TH], 1.0 / d, None, ALU.mult),
                waits=[stat_mm[KD - 1], prev_xn_last], inc="DV")
            t_msq = p.step("scalar", lambda e: e.activation(
                out=var_s[:], in_=m_s[:], func=AF.Square),
                waits=[t_m, prev_inv_tok], inc="AC")
            t_var = p.step("vector", lambda e: e.scalar_tensor_tensor(
                var_s[:], ps[:, 2048:2048 + TH], 1.0 / d, var_s[:],
                ALU.mult, ALU.subtract), waits=[t_msq], inc="DV")
            t_ln = p.step("scalar", lambda e: e.activation(
                out=inv_s[:], in_=var_s[:], func=AF.Ln, bias=eps_s[:, 0:1]),
                waits=[t_var, t_eps], inc="AC")
            t_inv = p.step("scalar", lambda e: e.activation(
                out=inv_s[:], in_=inv_s[:], func=AF.Exp, scale=-0.5),
                waits=[t_ln], inc="AC")
            xn_done = [None] * KD
            for k in range(KD):
                xbuf = x_s[:, (k % 2) * TH:(k % 2) * TH + TH]
                t_x = p.dma("sync", xbuf,
                            xT[k * 128:(k + 1) * 128, co:co + TH],
                            waits=[xn_done[k - 2]] if k >= 2 else [t_inv])
                t_sub = p.step("vector", lambda e, xb=xbuf: e.tensor_sub(
                    tmp_s[:], xb, m_s[:]),
                    waits=[t_x, t_inv] + ([xn_done[k - 1]] if k >= 1 else []), inc="DV")
                xn_done[k] = p.step("vector", lambda e, k=k: e.tensor_mul(
                    xn_s[:, k * TH:(k + 1) * TH], tmp_s[:], inv_s[:]),
                    waits=[t_sub], inc="DV")
            t_xn = xn_done[KD - 1]
            xn_last_tok = t_xn
            inv_tok = t_inv
            mm1_done = [None] * KH
            evg_done = [None] * KH
            evh_done = [None] * KH
            h_st = [None] * KH
            for m in range(KH):
                slot = m % 2
                gbuf = wgg_s[:, slot * KD * 128:(slot + 1) * KD * 128]
                ubuf = wug_s[:, slot * KD * 128:(slot + 1) * KD * 128]
                gsrc = bass.AP(wgTb.tensor, m * 128, [[H, 128], [128 * H, KD], [1, 128]])
                usrc = bass.AP(wuTb.tensor, m * 128, [[H, 128], [128 * H, KD], [1, 128]])
                wguard = [mm1_done[m - 2]] if m >= 2 else []
                t_wg = p.dma("gpsimd", gbuf.rearrange("p (kc o) -> p kc o", kc=KD),
                             gsrc, waits=wguard)
                t_wu = p.dma("gpsimd", ubuf.rearrange("p (kc o) -> p kc o", kc=KD),
                             usrc, waits=wguard)
                mm_last = None
                for k in range(KD):
                    for n in range(NT):
                        p.step("tensor", lambda e, k=k, n=n, gbuf=gbuf: e.matmul(
                            ps[:, n * 512:(n + 1) * 512],
                            gbuf[:, k * 128:(k + 1) * 128],
                            xn_s[:, k * TH + n * 512:k * TH + (n + 1) * 512],
                            start=(k == 0), stop=(k == KD - 1)),
                            waits=([t_wg, t_xn, t_m, t_var] +
                                   ([evg_done[m - 1]] if m >= 1 else [])) if (k == 0 and n == 0) else [],
                            inc=None)
                        mm_last = p.step("tensor", lambda e, k=k, n=n, ubuf=ubuf: e.matmul(
                            ps[:, 2048 + n * 512:2048 + (n + 1) * 512],
                            ubuf[:, k * 128:(k + 1) * 128],
                            xn_s[:, k * TH + n * 512:k * TH + (n + 1) * 512],
                            start=(k == 0), stop=(k == KD - 1)),
                            waits=([t_wu] + ([evh_done[m - 1]] if m >= 1 else [])) if (k == 0 and n == 0) else [],
                            inc="PE" if (k == KD - 1 and n == NT - 1) else None)
                mm1_done[m] = mm_last
                evg_done[m] = p.step("scalar", lambda e: e.activation(
                    out=sg_s[:], in_=ps[:, 0:TH], func=AF.Silu),
                    waits=[mm_last], inc="AC")
                evh_done[m] = p.step("vector", lambda e: e.tensor_mul(
                    ev_s[:], sg_s[:], ps[:, 2048:2048 + TH]),
                    waits=[evg_done[m], h_st[m - 1]], inc="DV")
                h_st[m] = p.dma("sync", hbuf[m * 128:(m + 1) * 128, :], ev_s[:],
                                waits=[evh_done[m]])
            NXO = 4
            prev_guard = evh_done[KH - 1]
            o_hist = []
            o_done2 = []
            mm2_done = {}
            for blk in range(KD // 2):
                o2s = [blk * 2, blk * 2 + 1]
                mm_last = None
                for k2 in range(KH):
                    wslot = k2 % 2
                    wbuf = w1g_s[:, wslot * 256:(wslot + 1) * 256]
                    hslot = k2 % 2
                    hb = h_s[:, hslot * TH:(hslot + 1) * TH]
                    wg = [mm2_done[(blk, k2 - 2)]] if k2 >= 2 else \
                         ([mm2_done[(blk - 1, KH - 1)]] if blk >= 1 else [])
                    t_wl = p.dma("gpsimd", wbuf,
                                 wdTb[k2 * 128:(k2 + 1) * 128, blk * 256:(blk + 1) * 256],
                                 waits=wg)
                    t_hl = p.dma("scalar", hb, hbuf[k2 * 128:(k2 + 1) * 128, :],
                                 waits=[h_st[k2]] + wg)
                    for i in range(2):
                        for n in range(NT):
                            slotp = i * NT + n
                            isl = (i == 1 and n == NT - 1)
                            mm_last = p.step("tensor", lambda e, i=i, n=n, k2=k2, slotp=slotp, wbuf=wbuf, hb=hb: e.matmul(
                                ps[:, slotp * 512:(slotp + 1) * 512],
                                wbuf[:, i * 128:(i + 1) * 128],
                                hb[:, n * 512:(n + 1) * 512],
                                start=(k2 == 0), stop=(k2 == KH - 1)),
                                waits=([t_wl, t_hl, prev_guard] if (i == 0 and n == 0) else []),
                                inc="PE" if isl else None)
                    mm2_done[(blk, k2)] = mm_last
                last_ac = None
                for i, o2 in enumerate(o2s):
                    for n in range(NT):
                        slotp = i * NT + n
                        j = len(o_hist)
                        xrb = xr_s[:, (j % NXO) * 512:(j % NXO) * 512 + 512]
                        orb = o_s[:, (j % NXO) * 512:(j % NXO) * 512 + 512]
                        t_xr = p.dma("sync", xrb,
                                     xT[o2 * 128:(o2 + 1) * 128, co + n * 512:co + (n + 1) * 512],
                                     waits=[o_hist[j - NXO]] if j >= NXO else [])
                        t1b = t1_s[:, (j % 2) * 512:(j % 2) * 512 + 512]
                        t_t1 = p.step("scalar", lambda e, slotp=slotp, t1b=t1b: e.activation(
                            out=t1b, in_=ps[:, slotp * 512:(slotp + 1) * 512],
                            func=AF.Identity, scale=0.5),
                            waits=[mm_last] + ([o_done2[j - 2]] if j >= 2 else []), inc="AC")
                        t_o = p.step("vector", lambda e, orb=orb, xrb=xrb, t1b=t1b: e.tensor_add(
                            orb, t1b, xrb),
                            waits=[t_t1, t_xr] + ([o_hist[j - NXO]] if j >= NXO else []),
                            inc="DV")
                        t_st = p.dma("sync",
                                     x1T[o2 * 128:(o2 + 1) * 128, co + n * 512:co + (n + 1) * 512],
                                     orb, waits=[t_o])
                        o_hist.append(t_st)
                        o_done2.append(t_o)
                        last_ac = t_t1
                prev_guard = last_ac
            phase_guards.append(list(o_hist))

        final_waits = []
        for toks in phase_guards:
            final_waits += toks[-Prog.NSEM:]
        p.step("gpsimd", lambda e: e.nop(), waits=final_waits)
        block = es.enter_context(nc.Block())
        p.realize(block)
    return nc



def build_ghost():
    nc = bass.Bass("TRN2", target_bir_lowering=False)
    lut = nc.dram_tensor("lut", [1, 16], F32, kind="ExternalInput").ap()
    gidxT = nc.dram_tensor("gidxT", [D, GS], I32, kind="ExternalInput").ap()
    uidxT = nc.dram_tensor("uidxT", [D, GS], I32, kind="ExternalInput").ap()
    didxT = nc.dram_tensor("didxT", [GH, DS], I32, kind="ExternalInput").ap()
    wgT = nc.dram_tensor("wgT", [D, GS], BF16, kind="ExternalOutput").ap()
    wuT = nc.dram_tensor("wuT", [D, GS], BF16, kind="ExternalOutput").ap()
    wdT = nc.dram_tensor("wdT", [GH, DS], BF16, kind="ExternalOutput").ap()

    with contextlib.ExitStack() as es:
        sb = lambda n, sh, dt: es.enter_context(nc.sbuf_tensor(n, sh, dt))
        idx_s = sb("idx_s", [128, 2 * GS], I32)
        idf_s = sb("idf_s", [128, 2 * GS], F32)
        term_s = sb("term_s", [128, GS], F32)
        acc_s = sb("acc_s", [128, GS], F32)
        ob_s = sb("ob_s", [128, 2 * GS], BF16)
        lut_s = sb("lut_s", [128, 16], F32)
        p = Prog(nc, es)

        lsrc = bass.AP(lut.tensor, 0, [[0, 128], [1, 16]])
        t_lut = p.dma("sync", lut_s[:], lsrc)

        tiles = ([(gidxT, wgT, r) for r in range(D // 128)] +
                 [(uidxT, wuT, r) for r in range(D // 128)] +
                 [(didxT, wdT, r) for r in range(GH // 128)])
        prev_cast = {}
        prev_dve = {}
        prev_st = {}
        for i, (src_t, dst_t, r) in enumerate(tiles):
            F = src_t.shape[1]
            sl = i % 2
            ibuf = idx_s[:, sl * GS:sl * GS + F]
            fbuf = idf_s[:, sl * GS:sl * GS + F]
            obuf = ob_s[:, sl * GS:sl * GS + F]
            t_i = p.dma("sync", ibuf, src_t[r * 128:(r + 1) * 128, :],
                        waits=[prev_dve.get(i - 2)])
            t_f = p.step("vector", lambda e, ibuf=ibuf, fbuf=fbuf: e.tensor_copy(fbuf, ibuf),
                         waits=[t_i, prev_dve.get(i - 2)], inc="DV")
            t_acc = None
            for v in range(16):
                t_t = p.step("vector", lambda e, fbuf=fbuf, v=v, F=F: e.tensor_scalar(
                    term_s[:, 0:F], fbuf, float(v), lut_s[:, v:v + 1],
                    ALU.is_equal, ALU.mult),
                    waits=[t_f, t_lut, prev_cast.get(i - 1)] if v == 0 else [], inc="DV")
                if v == 0:
                    t_acc = p.step("vector", lambda e, F=F: e.tensor_copy(
                        acc_s[:, 0:F], term_s[:, 0:F]), inc="DV")
                else:
                    t_acc = p.step("vector", lambda e, F=F: e.tensor_add(
                        acc_s[:, 0:F], acc_s[:, 0:F], term_s[:, 0:F]), inc="DV")
            t_c = p.step("scalar", lambda e, obuf=obuf, F=F: e.activation(
                out=obuf, in_=acc_s[:, 0:F], func=AF.Copy),
                waits=[t_acc, prev_st.get(i - 2)], inc="AC")
            prev_cast[i] = t_c
            t_s = p.dma("sync", dst_t[r * 128:(r + 1) * 128, :], obuf, waits=[t_c])
            prev_st[i] = t_s
            prev_dve[i] = t_acc
            prev_dve[i] = t_c
        p.step("gpsimd", lambda e: e.nop(),
               waits=[prev_st[len(tiles) - 1], prev_st[len(tiles) - 2]])
        block = es.enter_context(nc.Block())
        p.realize(block)
    return nc



def _fm(a):
    return np.ascontiguousarray(a.T)


def _colsplit(b):
    return np.ascontiguousarray(b.reshape(-1, 128).T)


def run_ffn0(x_slices, W0, b0, W1, b1, nc_prog=None):
    H = W0.shape[0]
    nc = nc_prog if nc_prog is not None else build_ffn0(H=H)
    w0T = np.ascontiguousarray(W0.T)
    w1T = np.ascontiguousarray(W1.T)
    b0r = _colsplit(b0)
    b1r = _colsplit(b1)
    ims = [{"xT": _fm(xs), "w0T": w0T, "b0r": b0r, "w1T": w1T, "b1r": b1r}
           for xs in x_slices]
    res = run_bass_kernel_spmd(nc, ims, core_ids=list(range(NCORES)))
    return [np.asarray(r["x1T"]).T for r in res.results]



def _tok_slices(x):
    return [np.concatenate([x[b, i * SLOC:(i + 1) * SLOC] for b in range(B)], 0)
            for i in range(NCORES)]


def _run(nc, ims, outs):
    res = run_bass_kernel_spmd(nc, ims, core_ids=list(range(NCORES)))
    return [{k: np.asarray(res.results[c][k]) for k in outs} for c in range(NCORES)]


def kernel(**inputs):
    import ml_dtypes
    f32 = np.float32
    x0 = np.ascontiguousarray(inputs["L0_attn_hidden_states"], dtype=f32)
    W0 = inputs["L0_ffn_W0"]; b0 = inputs["L0_ffn_bias0"]
    W1 = inputs["L0_ffn_W1"]; b1 = inputs["L0_ffn_bias1"]
    ipw0 = inputs["L0_attn_in_proj_w"]; cw0 = inputs["L0_attn_conv_w"]
    cb0 = inputs["L0_attn_conv_b"]; alog0 = inputs["L0_attn_A_log"]
    dtb0 = inputs["L0_attn_dt_bias"]; D0 = inputs["L0_attn_D"]
    nw0 = inputs["L0_attn_norm_w"]; opw0 = inputs["L0_attn_out_proj_w"]
    ipw1 = inputs["L1_attn_in_proj_w"]; cw1 = inputs["L1_attn_conv_w"]
    cb1 = inputs["L1_attn_conv_b"]; xpw1 = inputs["L1_attn_x_proj_w"]
    dtw1 = inputs["L1_attn_dt_proj_w"]; dtpb1 = inputs["L1_attn_dt_proj_b"]
    alog1 = inputs["L1_attn_A_log"]; Dp1 = inputs["L1_attn_D_param"]
    opw1 = inputs["L1_attn_out_proj_w"]
    lut = inputs["L1_ffn_lut"]; gidx = inputs["L1_ffn_gate_idx"]
    uidx = inputs["L1_ffn_up_idx"]; didx = inputs["L1_ffn_down_idx"]

    nc_ffn0 = build_ffn0()
    nc_ghost = build_ghost()
    nc_l0pre = build_l0pre()
    nc_l0mid = build_l0mid()
    nc_l0fin = build_l0fin()
    nc_gffn = build_gffn()
    nc_l1pre = build_l1pre()
    nc_scan = build_scan()
    nc_l1tail = build_l1tail()

    w0T = np.ascontiguousarray(W0.T); b0r = _colsplit(b0)
    w1T = np.ascontiguousarray(W1.T); b1r = _colsplit(b1)
    x_slices = [_fm(s) for s in _tok_slices(x0)]
    ims = [{"xT": xs, "w0T": w0T, "b0r": b0r, "w1T": w1T, "b1r": b1r}
           for xs in x_slices]
    r = _run(nc_ffn0, ims, ["x1T"])
    x1 = [rr["x1T"] for rr in r]

    lutr = np.ascontiguousarray(lut.reshape(1, 16).astype(f32))
    ims = [{"lut": lutr,
            "gidxT": np.ascontiguousarray(gidx[c * GS:(c + 1) * GS].T),
            "uidxT": np.ascontiguousarray(uidx[c * GS:(c + 1) * GS].T),
            "didxT": np.ascontiguousarray(didx.T[:, c * DS:(c + 1) * DS])}
           for c in range(NCORES)]
    r = _run(nc_ghost, ims, ["wgT", "wuT", "wdT"])
    wgTb = np.concatenate([rr["wgT"] for rr in r], 1)
    wuTb = np.concatenate([rr["wuT"] for rr in r], 1)
    wdTb = np.concatenate([rr["wdT"] for rr in r], 1)

    ipw0T = np.ascontiguousarray(ipw0.T)
    ims = [{"xT": x1[c], "ipwT": ipw0T} for c in range(NCORES)]
    r = _run(nc_l0pre, ims, ["hBCp", "dtraw", "sgtm"])
    hBCp = [rr["hBCp"] for rr in r]
    dtraw = [rr["dtraw"] for rr in r]
    sgtm = [rr["sgtm"] for rr in r]

    KB = CONV_DIM // 128
    cwr = np.ascontiguousarray(cw0[:, 0, :].reshape(KB, 128, 4).transpose(1, 0, 2).reshape(128, KB * 4))
    cbr = _colsplit(cb0)
    TH = SLOC
    ims = []
    for c in range(NCORES):
        parts = []
        for b in range(B):
            if c == 0:
                halo = np.zeros((CONV_DIM, 3), dtype=ml_dtypes.bfloat16)
            else:
                halo = hBCp[c - 1][:, (b + 1) * TH - 3:(b + 1) * TH]
            parts += [halo, hBCp[c][:, b * TH:(b + 1) * TH]]
        ims.append({"hBCh": np.ascontiguousarray(np.concatenate(parts, 1)),
                    "dtraw": dtraw[c], "cwr": cwr, "cbr": cbr,
                    "alog": np.ascontiguousarray(alog0.reshape(32, 1).astype(f32)),
                    "dtb": np.ascontiguousarray(dtb0.reshape(32, 1).astype(f32))})
    r = _run(nc_l0mid, ims, ["xstm", "xdtm", "Bfm", "Cfm", "sdtm", "ydiag",
                             "dacc", "slocal", "send"])
    mid = r

    opw0T = np.ascontiguousarray(opw0.T)
    NCc = TH // CHUNK
    sends = [np.asarray(mid[c]["send"]).reshape(2, 128, HEADS * 128) for c in range(NCORES)]
    daccs = [np.asarray(mid[c]["dacc"]).reshape(2, 32, NCc) for c in range(NCORES)]
    ims = []
    for c in range(NCORES):
        sstack = np.zeros((8, 2, 128, HEADS * 128), f32)
        gstack = np.ones((8, 2, 32, 8), f32)
        for j in range(c):
            sstack[8 - c + j] = sends[j]
            gstack[8 - c + j] = daccs[j][:, :, :8] if NCc >= 8 else np.pad(
                daccs[j], ((0, 0), (0, 0), (0, 8 - NCc)), constant_values=1.0)
        gstin = np.ascontiguousarray(gstack.transpose(1, 2, 0, 3).reshape(64, 64))
        dcsin = np.ascontiguousarray(daccs[c].reshape(64, NCc))
        ims.append({"ydiag": mid[c]["ydiag"], "xstm": mid[c]["xstm"],
                    "sgtm": sgtm[c], "Cfm": mid[c]["Cfm"], "sdtm": mid[c]["sdtm"],
                    "x1T": x1[c], "opwT": opw0T,
                    "normw": np.ascontiguousarray(nw0.reshape(1, -1).astype(f32)),
                    "dvec": np.ascontiguousarray(D0.reshape(1, -1).astype(f32)),
                    "sstack": sstack.reshape(8 * 2 * 128, HEADS * 128),
                    "gstin": gstin, "dcsin": dcsin,
                    "slocal": np.asarray(mid[c]["slocal"]).reshape(2 * NCc * 128, HEADS * 128)})
    r = _run(nc_l0fin, ims, ["x2T"])
    x2 = [rr["x2T"] for rr in r]

    ims = [{"xT": x2[c], "w0T": w0T, "b0r": b0r, "w1T": w1T, "b1r": b1r}
           for c in range(NCORES)]
    r = _run(nc_ffn0, ims, ["x1T"])
    x3 = [rr["x1T"] for rr in r]

    ims = [{"xT": x3[c], "wgTb": wgTb, "wuTb": wuTb, "wdTb": wdTb}
           for c in range(NCORES)]
    r = _run(nc_gffn, ims, ["x1T"])
    x4 = [rr["x1T"] for rr in r]

    KU = DL1 // 128
    ipw1T = np.ascontiguousarray(ipw1.T)
    cwr1 = np.ascontiguousarray(cw1[:, 0, :].reshape(KU, 128, 4).transpose(1, 0, 2).reshape(128, KU * 4))
    ims = []
    for c in range(NCORES):
        xp = np.zeros((D, 2 * (512 + TH)), f32)
        for b in range(B):
            blk = b * (512 + TH)
            xp[:, blk + 512:blk + 512 + TH] = x4[c][:, b * TH:(b + 1) * TH]
            if c > 0:
                xp[:, blk + 509:blk + 512] = x4[c - 1][:, (b + 1) * TH - 3:(b + 1) * TH]
        ims.append({"xT": xp, "ipwT": ipw1T,
                    "xpwT": np.ascontiguousarray(xpw1.T),
                    "dtwT": np.ascontiguousarray(dtw1.T),
                    "dtbr": _colsplit(dtpb1),
                    "cwr": cwr1, "cbr": _colsplit(cb1)})
    r = _run(nc_l1pre, ims, ["ucb", "dtfo", "dub", "sgb", "bcs"])
    pre = r

    DSL = 512
    def reshard(key, dt):
        full = [np.concatenate([np.asarray(pre[c][key])[:, b * TH:(b + 1) * TH]
                                for c in range(NCORES)], 1) for b in range(B)]
        return [np.ascontiguousarray(np.concatenate(
            [full[b][i * DSL:(i + 1) * DSL] for b in range(B)], 0).astype(dt))
            for i in range(NCORES)]
    dt_d = reshard("dtfo", f32)
    du_d = reshard("dub", ml_dtypes.bfloat16)
    u_d = reshard("ucb", ml_dtypes.bfloat16)
    sg_d = reshard("sgb", ml_dtypes.bfloat16)
    bcf = np.concatenate([np.concatenate(
        [np.asarray(pre[c]["bcs"])[:, b * TH:(b + 1) * TH] for c in range(NCORES)], 1)
        for b in range(B)], 0)
    bcf = np.ascontiguousarray(bcf)
    pp, qq = np.meshgrid(np.arange(128), np.arange(64), indexing="ij")
    ims = []
    for c in range(NCORES):
        al = alog1[c * DSL:(c + 1) * DSL]
        alogr = np.ascontiguousarray(al[qq * 8 + pp // 16, pp % 16].astype(f32))
        ims.append({"dtc": dt_d[c], "duc": du_d[c], "uc": u_d[c], "sgc": sg_d[c],
                    "alogr": alogr, "dparr": _colsplit(Dp1[c * DSL:(c + 1) * DSL]),
                    "bcf": bcf})
    r = _run(nc_scan, ims, ["zc"])
    zc = r

    zfull = [np.concatenate([np.asarray(zc[c]["zc"])[b * DSL:(b + 1) * DSL]
                             for c in range(NCORES)], 0) for b in range(B)]
    opw1T = np.ascontiguousarray(opw1.T)
    ims = []
    for c in range(NCORES):
        zT = np.concatenate([zfull[b][:, c * TH * 0 + c * SLOC:(c + 1) * SLOC]
                             for b in range(B)], 1)
        ims.append({"zT": np.ascontiguousarray(zT), "x4T": x4[c], "opwT": opw1T})
    r = _run(nc_l1tail, ims, ["x5T"])
    x5 = [rr["x5T"] for rr in r]

    ims = [{"xT": x5[c], "wgTb": wgTb, "wuTb": wuTb, "wdTb": wdTb}
           for c in range(NCORES)]
    r = _run(nc_gffn, ims, ["x1T"])
    x6 = [rr["x1T"] for rr in r]

    out = np.empty((B, S, D), f32)
    for c in range(NCORES):
        xt = np.asarray(x6[c]).T
        for b in range(B):
            out[b, c * SLOC:(c + 1) * SLOC] = xt[b * SLOC:(b + 1) * SLOC]
    return out
